# version 14
# baseline (speedup 1.0000x reference)
import numpy as np
import ml_dtypes
import concourse.bass as bass
import concourse.mybir as mybir
from concourse.bass_utils import run_bass_kernel_spmd
from concourse.alu_op_type import AluOpType as Op

F32 = mybir.dt.float32
BF16 = mybir.dt.bfloat16
AF = mybir.ActivationFunctionType
BF = ml_dtypes.bfloat16

T = 4096
E = 1024
H, HD = 16, 64
L = 4
HIDDEN = 4096
NEXP, DEXP, DSH = 8, 512, 1024
EPS = 1e-5
NCORE = 8

CH1 = 256
NCH1 = T // CH1
CH2 = 512
NCH2 = T // CH2
CHB = 256
THALF = T // 2
RINV = 1.0 / E

_CONSTS = None


def host_constants():
    global _CONSTS
    if _CONSTS is not None:
        return _CONSTS
    n64 = np.arange(64)
    W64 = np.exp(-2j * np.pi * np.outer(n64, n64) / 64)
    n16 = np.arange(16)
    W16 = np.exp(-2j * np.pi * np.outer(n16, n16) / 16)
    W64i = np.conj(W64) / 64.0
    W16i = np.conj(W16) / 16.0

    Sd = np.zeros((128, 128), np.complex128)
    for hpar in range(2):
        for d in range(64):
            for kd in range(64):
                Sd[hpar * 64 + d, ((kd >> 3) * 2 + hpar) * 8 + (kd & 7)] = W64[d, kd]
    Sh = np.zeros((128, 128), np.complex128)
    Shi = np.zeros((128, 128), np.complex128)
    for Bq in range(8):
        for kh in range(16):
            for tau in range(8):
                for hpar in range(2):
                    Sh[tau * 16 + Bq * 2 + hpar, kh * 8 + Bq] = W16[2 * tau + hpar, kh]
                    Shi[kh * 8 + Bq, (Bq * 2 + hpar) * 8 + tau] = W16i[kh, 2 * tau + hpar]
    Sdi = np.zeros((128, 128), np.complex128)
    for Bq in range(8):
        for A in range(8):
            for hpar in range(2):
                for d in range(64):
                    Sdi[A * 16 + Bq * 2 + hpar, hpar * 64 + d] = W64i[8 * Bq + A, d]

    ks = np.fft.fftfreq(T).astype(np.float64)[:, None, None]
    khf = np.fft.fftfreq(H).astype(np.float64)[None, :, None]
    kdf = np.fft.fftfreq(HD).astype(np.float64)[None, None, :]
    km = np.sqrt(ks ** 2 + khf ** 2 + kdf ** 2 + 1e-10)
    Filt = np.exp(1j * np.arctan(np.log(km + 1e-10)))
    C = np.zeros((128, 8, T), np.float32)
    Sg = np.zeros((128, 8, T), np.float32)
    for Bq in range(8):
        for kh in range(16):
            p = kh * 8 + Bq
            for a in range(8):
                C[p, a, :] = Filt[:, kh, 8 * Bq + a].real.astype(np.float32)
                Sg[p, a, :] = Filt[:, kh, 8 * Bq + a].imag.astype(np.float32)

    lam = np.arange(T, dtype=np.float32) / T
    tt = np.arange(T, dtype=np.float32) * 0.1
    mod = (1.0 + 0.1 * np.sin(tt + lam) * np.cos(tt - 2.0 * lam + 0.5 * lam ** 2))

    cs = {
        "c_sd_re": Sd.real, "c_sd_im": Sd.imag,
        "c_sh_re": Sh.real, "c_sh_im": Sh.imag, "c_sh_imn": -Sh.imag,
        "c_shi_re": Shi.real, "c_shi_im": Shi.imag, "c_shi_imn": -Shi.imag,
        "c_sdi_re": Sdi.real, "c_sdi_imn": -Sdi.imag,
    }
    cs = {k: v.astype(BF) for k, v in cs.items()}
    cs["c_filt_c"] = C.astype(BF)
    cs["c_filt_s"] = Sg.astype(BF)
    cs["c_mod"] = mod.astype(np.float32).reshape(1, T)
    cs["c_id_bf"] = np.eye(128, dtype=BF)
    cs["c_id_f32"] = np.eye(128, dtype=np.float32)
    _CONSTS = cs
    return cs


class _OpRec:
    __slots__ = ("engine", "emit", "deps", "rawdeps", "inc", "need_inc", "cum",
                 "sem", "prev_slot", "dense", "eidx")

    def __init__(self, engine, emit, deps, rawdeps, inc, dense, eidx):
        self.engine = engine
        self.emit = emit
        self.deps = deps
        self.rawdeps = rawdeps
        self.inc = inc
        self.need_inc = False
        self.cum = 0
        self.sem = engine
        self.prev_slot = None
        self.dense = dense
        self.eidx = eidx


def _dense_ap(ap):
    try:
        from concourse.bass_primitives import MemorySpace
        if ap.space == MemorySpace.PSUM:
            return False
    except Exception:
        return False
    try:
        a = ap.opt().ap
    except Exception:
        try:
            a = ap.ap
        except Exception:
            return False
    dims = [d for d in a[1:] if d[1] > 1]
    dims.sort(key=lambda d: -d[0])
    run = 1
    for step, count in reversed(dims):
        if step != run:
            return False
        run = step * count
    return True


ENGS = ("pe", "dve", "act", "syncq", "gpsq")
SYNC_SAME = ("dve",)
NSLOT = 8
DMASEMS = tuple("%s%d" % (q, i) for q in ("syncq", "gpsq") for i in range(NSLOT))
ALLSEMS = ENGS + DMASEMS


class Sched:
    def __init__(self):
        self.ops = []
        self.last_write = {}
        self.readers = {}
        self.ecount = {}

    def __init_extra(self):
        pass

    def op(self, engine, emit, reads=(), writes=(), dma=False, aps=()):
        reads = tuple(reads) + ("EP",)
        rawdeps = set()
        wdeps = set()
        for k in reads:
            w = self.last_write.get(k)
            if w is not None:
                rawdeps.add(w)
        for k in writes:
            w = self.last_write.get(k)
            if w is not None:
                wdeps.add(w)
            rd = self.readers.get(k)
            if rd:
                wdeps.update(rd)
        rawdeps -= wdeps
        i = len(self.ops)
        dense = all(_dense_ap(a) for a in aps) if aps else False
        eidx = self.ecount.get(engine, 0)
        self.ecount[engine] = eidx + 1
        self.ops.append(_OpRec(engine, emit, rawdeps | wdeps, rawdeps,
                               16 if dma else 1, dense, eidx))
        for k in reads:
            self.readers.setdefault(k, []).append(i)
        for k in writes:
            self.last_write[k] = i
            self.readers[k] = []
        return i

    def fence(self, emit):
        self.op("dve", emit, reads=(), writes=("EP",))

    def finalize(self):
        slot_rr = {"syncq": 0, "gpsq": 0}
        slot_last = {s: None for s in DMASEMS}
        for i, o in enumerate(self.ops):
            if o.inc == 16:
                s = "%s%d" % (o.engine, slot_rr[o.engine])
                slot_rr[o.engine] = (slot_rr[o.engine] + 1) % NSLOT
                o.sem = s
                o.prev_slot = slot_last[s]
                slot_last[s] = i
        def skip_same(dep, o, d):
            if dep.engine != o.engine or dep.inc != 1:
                return False
            if dep.engine not in SYNC_SAME:
                return True
            if d not in o.rawdeps:
                return False
            return o.eidx - dep.eidx >= 2

        for o in self.ops:
            if o.inc == 16:
                o.need_inc = True
            for d in o.deps:
                dep = self.ops[d]
                if skip_same(dep, o, d):
                    continue
                dep.need_inc = True
        cum = {e: 0 for e in ALLSEMS}
        for o in self.ops:
            if o.need_inc:
                cum[o.sem] += o.inc
                o.cum = cum[o.sem]
        emit_lists = {e: [] for e in ENGS}
        waited = {e: {s: 0 for s in ALLSEMS} for e in ENGS}
        for o in self.ops:
            waits = {}
            for d in o.deps:
                dep = self.ops[d]
                if skip_same(dep, o, d):
                    continue
                if waits.get(dep.sem, 0) < dep.cum:
                    waits[dep.sem] = dep.cum
            if o.prev_slot is not None:
                dep = self.ops[o.prev_slot]
                if waits.get(dep.sem, 0) < dep.cum:
                    waits[dep.sem] = dep.cum
            wl = []
            for se, v in waits.items():
                if waited[o.engine][se] < v:
                    waited[o.engine][se] = v
                    wl.append((se, v))
            emit_lists[o.engine].append((wl, o))
        return emit_lists, cum


_PROG = None
DEBUG = False


def build_program():
    global _PROG
    if _PROG is not None and not DEBUG:
        return _PROG
    nc = bass.Bass("TRN2", target_bir_lowering=False)
    S = Sched()
    cs = host_constants()

    dr = {}
    dr["x"] = nc.dram_tensor("x", [T, E], F32, kind="ExternalInput").ap()
    out_dr = nc.dram_tensor("out", [T, E], F32, kind="ExternalOutput").ap()
    for nm, shp in [
        ("ln1_w", [L, E]), ("ln1_b", [L, E]),
        ("attn_w", [L, E, 3 * E]), ("attn_b", [L, 3 * E]),
        ("aproj_w", [L, E, E]), ("aproj_b", [L, E]),
        ("ln2_w", [L, E]), ("ln2_b", [L, E]),
        ("fc_w", [L, E, HIDDEN]), ("fc_b", [L, HIDDEN]),
        ("mproj_w", [L, HIDDEN, E]), ("mproj_b", [L, E]),
        ("router", [E, NEXP]),
        ("eg", [NEXP, E, DEXP]), ("eu", [NEXP, E, DEXP]), ("ed", [NEXP, DEXP, E]),
        ("sg", [E, DSH]), ("su", [E, DSH]), ("sd", [DSH, E]),
        ("biasf", [128, 416]), ("routf", [128, 64]),
    ]:
        dr[nm] = nc.dram_tensor(nm, shp, F32, kind="ExternalInput").ap()
    for nm, arr in cs.items():
        dt = BF16 if arr.dtype == BF else F32
        dr[nm] = nc.dram_tensor(nm, list(arr.shape), dt, kind="ExternalInput").ap()
    bounce = nc.dram_tensor("bounce", [2, 2, NCH1, 16, 8, 8, CH1], BF16,
                            kind="Internal").ap()

    from contextlib import ExitStack
    es = ExitStack()
    with es:
        R = es.enter_context(nc.sbuf_tensor("R", [128, 8, T], BF16))
        GEN = es.enter_context(nc.sbuf_tensor("GEN", [128, 62464], BF16))
        GENF = es.enter_context(nc.sbuf_tensor("GENF", [128, 2048], F32))
        SM = es.enter_context(nc.sbuf_tensor("SM", [128, 2048], F32))
        SMB = es.enter_context(nc.sbuf_tensor("SMB", [128, 2664], BF16))
        pSTAT = es.enter_context(nc.psum_tensor("pSTAT", [128, 512], F32))
        pA = es.enter_context(nc.psum_tensor("pA", [128, 512], F32))
        pB = es.enter_context(nc.psum_tensor("pB", [128, 512], F32))
        pTP = es.enter_context(nc.psum_tensor("pTP", [128, 1024], BF16))
        pBIG = es.enter_context(nc.psum_tensor("pBIG", [128, 2048], F32))
        sems = {e: es.enter_context(nc.semaphore("sem_" + e)) for e in ALLSEMS}
        block = es.enter_context(nc.Block())

        def gv(off, *dims):
            n = int(np.prod(dims))
            apx = GEN[:, off:off + n]
            if len(dims) > 1:
                apx = apx.rearrange("p (a b) -> p a b", a=dims[0], b=dims[1])
            return apx

        def grow(off, n):
            return GEN[0:1, off:off + n]

        vWQ = gv(0, 8, 1024); vWK = gv(8192, 8, 1024); vWV = gv(16384, 8, 1024)
        vXH = gv(24576, 8, CH1)
        vQ2P = [gv(26624, 8, CH1), gv(28672, 8, CH1)]
        vK2P = [gv(30720, 8, CH1), gv(32768, 8, CH1)]
        vV2P = [gv(34816, 8, CH1), gv(36864, 8, CH1)]
        vRQ = gv(38912, 4, CH1); vRK = gv(39936, 4, CH1); vRV = gv(40960, 4, CH1)
        vDQ = gv(41984, 4, CH1); vDK = gv(43008, 4, CH1); vDV = gv(44032, 4, CH1)
        vCQ = gv(45056, 4, CH1); vCK = gv(46080, 4, CH1); vCV = gv(47104, 4, CH1)
        vT0 = gv(48128, 4, CH1); vT1 = gv(49152, 4, CH1)
        vZ = gv(50176, 8, CH1)
        vZDRE = gv(52224, 4, CH1); vZDIM = gv(53248, 4, CH1)
        MUROW_O, RSTDROW_O = 54272, 58368
        vLNT = vT0[:, 0:1, :]
        vHQ = SMB[:, 2128:2136].rearrange("p (a b) -> p a b", a=8)
        vHK = SMB[:, 2136:2144].rearrange("p (a b) -> p a b", a=8)
        vHV = SMB[:, 2144:2152].rearrange("p (a b) -> p a b", a=8)
        vWP = gv(0, 8, 1024)
        vFC2P = [gv(8192, 8, CH1), gv(10240, 8, CH1)]
        vFS2P = [gv(12288, 8, CH1), gv(14336, 8, CH1)]
        vLHRE2P = [gv(16384, 8, CH1), gv(18432, 8, CH1)]
        vLHIM2P = [gv(20480, 8, CH1), gv(22528, 8, CH1)]
        vZPRE = gv(24576, 8, CH1); vZPIM = gv(26624, 8, CH1)
        vYRE = gv(28672, 8, CH1); vYIM = gv(30720, 8, CH1)
        vYOUT2P = [gv(32768, 8, CH1), gv(34816, 8, CH1)]
        vMODB = gv(36864, 1, CH1)
        vFTMP = gv(37120, 2, CH1)
        MODROW_O = 39424
        vFCW = gv(0, 8, 2048); vMPW = gv(16384, 16, 1024)
        vXH2H = gv(32768, 8, THALF)
        MUROW2_O, RSTDROW2_O = 49152, 53248
        vH1 = gv(49152, 16, CH2)
        vLNT2 = gv(57344, 1, CH2)
        vXNH = gv(0, 8, THALF); vACC = gv(16384, 8, THALF)
        EW_O = [32768, 45056]
        vSGW = gv(32768, 8, 1024); vSUW = gv(40960, 8, 1024); vSDW = gv(49152, 8, 1024)
        COEFROW_O = 57344
        vCOEFROW = GEN[0:8, 57344:57344 + 2048]
        WORK_O = 59392
        vXNT = gv(59392, 1, 1024)
        vCOEFT = GEN[:, 60416:60416 + 128].rearrange("p (g c) -> p g c", g=16)
        vFSIL = gv(60544, 1, 128)
        vFPROD = gv(60672, 1, 128)
        vSGE = gv(59392, 2, 512); vHUE = gv(60416, 4, 512)
        vCOEFR0B = SMB[0:1, 2152:2664]
        vSGSH = gv(59392, 4, CHB)
        vHUSH = gv(60416, 8, CHB)

        sAB = SM[:, 0:96].rearrange("p (l m) -> p l m", l=L)
        sPB = SM[:, 96:128].rearrange("p (l m) -> p l m", l=L)
        sFB = SM[:, 128:256].rearrange("p (l m) -> p l m", l=L)
        sMB = SM[:, 256:288].rearrange("p (l m) -> p l m", l=L)
        sLN1W = SM[:, 288:320].rearrange("p (l m) -> p l m", l=L)
        sLN1B = SM[:, 320:352].rearrange("p (l m) -> p l m", l=L)
        sLN2W = SM[:, 352:384].rearrange("p (l m) -> p l m", l=L)
        sLN2B = SM[:, 384:416].rearrange("p (l m) -> p l m", l=L)
        sTOP = SM[:, 416:480].rearrange("p (t two) -> p t two", t=32)
        sBN = SM[:, 480:492]
        sLTOK = SM[:, 496:624].rearrange("p (g c) -> p g c", g=16)
        sSUMS = SM[:, 624:640]
        sM1 = SM[:, 640:656]
        sM2 = SM[:, 656:672]
        sCOEFP = SM[:, 672:800].rearrange("p (g c) -> p g c", g=16)
        sSUP = SM[:, 800:808]
        sLGT = SM[0:8, 808:1320]
        sIDF = SM[:, 1320:1448]
        sVAR1 = SM[0:1, 1448:1960]
        sTOPR = SM[:, 1960:1992]

        smb_names = ["c_sd_re", "c_sd_im", "c_sh_re", "c_sh_im", "c_sh_imn",
                     "c_shi_re", "c_shi_im", "c_shi_imn", "c_sdi_re", "c_sdi_imn"]
        bST = {nm: SMB[:, i * 128:(i + 1) * 128] for i, nm in enumerate(smb_names)}
        bIDB = SMB[:, 1280:1408]
        bONESR = SMB[0:1, 1408:1536]
        bONESC = SMB[:, 1536:1537]
        bROUT = SMB[:, 1540:1604].rearrange("p (k m) -> p k m", k=8)
        bNEGH = SMB[0:1, 1608:2120]

        def dma(q, outap, inap, reads, writes, **kw):
            S.op(q, lambda e, o=outap, i=inap, k=kw: e.dma_start(out=o, in_=i, **k),
                 reads, writes, dma=True)

        def pe_mm(outap, lhs, rhs, reads, writes, start=True, stop=True):
            S.op("pe", lambda e, o=outap, a=lhs, b=rhs, s1=start, s2=stop:
                 e.matmul(o, a, b, start=s1, stop=s2), reads, writes)

        def pe_tr(outap, inap, ident, reads, writes):
            S.op("pe", lambda e, o=outap, i=inap, d=ident: e.transpose(o, i, d),
                 reads, writes)

        def act(outap, inap, func, reads, writes, bias=None, scale=None, accum=None):
            kw = {}
            if bias is not None:
                kw["bias"] = bias
            if scale is not None:
                kw["scale"] = scale
            if accum is not None:
                kw["accum_out"] = accum
            S.op("act", lambda e, o=outap, i=inap, f=func, k=kw:
                 e.activation(o, i, f, **k), reads, writes)

        def tt(outap, a, bb, op, reads, writes):
            S.op("dve", lambda e, o=outap, x=a, y=bb, p=op: e.tensor_tensor(o, x, y, p),
                 reads, writes, aps=(outap, a, bb))

        def ts(outap, a, s1, s2, op0, op1, reads, writes):
            ptr = any(hasattr(v, "ap") for v in (s1, s2))
            S.op("dve", lambda e, o=outap, x=a, u=s1, v=s2, p0=op0, p1=op1:
                 e.tensor_scalar(o, x, u, v, p0, p1), reads, writes,
                 aps=() if ptr else (outap, a))

        def stt(outap, in0, scal, in1, op0, op1, reads, writes):
            ptr = hasattr(scal, "ap")
            S.op("dve", lambda e, o=outap, x=in0, s=scal, y=in1, p0=op0, p1=op1:
                 e.scalar_tensor_tensor(o, x, s, y, p0, p1), reads, writes,
                 aps=() if ptr else (outap, in0, in1))

        def gp_tt(outap, a, bb, op, reads, writes):
            S.op("gpsq", lambda e, o=outap, x=a, y=bb, p=op:
                 e.tensor_tensor(o, x, y, p), reads, writes)

        def cp(outap, inap, reads, writes, eng="dve"):
            S.op(eng, lambda e, o=outap, i=inap: e.tensor_copy(o, i), reads, writes,
                 aps=(outap, inap))

        def red(outap, inap, op, reads, writes):
            S.op("dve", lambda e, o=outap, i=inap, p=op:
                 e.tensor_reduce(o, i, mybir.AxisListType.X, p), reads, writes,
                 aps=(outap, inap))

        dbg_done = set()

        def tap(name, ap, reads):
            if not DEBUG or name in dbg_done:
                return
            dbg_done.add(name)
            shp = list(ap.shape)
            d = nc.dram_tensor("dbg_" + name, shp, ap.dtype, kind="ExternalOutput").ap()
            dma("syncq", d, ap, reads, ("dbg" + name,))

        for nm in smb_names:
            dma("syncq", bST[nm], dr[nm], (), (nm,))
        dma("syncq", bIDB, dr["c_id_bf"], (), ("idb",))
        dma("syncq", sIDF, dr["c_id_f32"], (), ("idf",))
        S.op("dve", lambda e: e.memset(bONESR, 1.0), (), ("onesr",))
        S.op("dve", lambda e: e.memset(bNEGH, -0.5), (), ("negh",))
        S.op("dve", lambda e: e.memset(bONESC, 1.0), (), ("onesc",))
        dma("gpsq", bROUT, dr["routf"].rearrange("p (k m) -> p k m", k=8),
            (), ("rout",))

        dma("syncq", SM[:, 0:416], dr["biasf"], (),
            ("ab", "pb", "fb", "mb", "lnv"))

        def top_ln_into(dst, dst_col0, ntiles, tok0, xn_tile_sink, tpx="I"):
            for tti in range(ntiles):
                xp = tti % 2
                xt = GENF[:, xp * 1024:(xp + 1) * 1024]
                xk = "xt%d" % xp
                dma("syncq", xt, dr["x"][tok0 + tti * 128: tok0 + (tti + 1) * 128, :],
                    (xk,), (xk,))
                red(sTOP[:, tti, 0:1], xt, Op.add, (xk,), ("top",))
                act(vXNT[:, 0, :], xt, AF.Square, (xk,), ("W1",))
                red(sTOP[:, tti, 1:2], vXNT[:, 0, :], Op.add, ("W1",), ("top",))
            vsum = sTOP[:, 0:ntiles, 0:1]
            vsq = sTOP[:, 0:ntiles, 1:2]
            ts(vsum, vsum, RINV, None, Op.mult, Op.bypass, ("top",), ("top",))
            ts(vsq, vsq, RINV, EPS, Op.mult, Op.add, ("top",), ("top",))
            tt(sTOPR[:, 0:ntiles], vsum, vsum, Op.mult, ("top",), ("rst",))
            tt(sTOPR[:, 0:ntiles], vsq, sTOPR[:, 0:ntiles], Op.subtract,
               ("top", "rst"), ("rst",))
            act(sTOPR[:, 0:ntiles], sTOPR[:, 0:ntiles], AF.Ln, ("rst",), ("rst",))
            act(sTOPR[:, 0:ntiles], sTOPR[:, 0:ntiles], AF.Exp, ("rst",), ("rst",),
                scale=-0.5)
            for tti in range(ntiles):
                xp = tti % 2
                xt = GENF[:, xp * 1024:(xp + 1) * 1024]
                xk = "xt%d" % xp
                dma("syncq", xt, dr["x"][tok0 + tti * 128: tok0 + (tti + 1) * 128, :],
                    (xk,), (xk,))
                ts(vXNT[:, 0, :], xt, sTOP[:, tti, 0:1], sTOPR[:, tti:tti + 1],
                   Op.subtract, Op.mult, (xk, "top", "rst"), ("W1",))
                for fc in range(8):
                    pe_tr(pTP[:, fc * 128:(fc + 1) * 128],
                          vXNT[:, 0, fc * 128:(fc + 1) * 128], bIDB,
                          ("W1", "idb"), ("pTP",))
                xn_tile_sink(tti, pTP[:, 0:1024].rearrange("p (a b) -> p a b", a=8))
            if DEBUG:
                tap("TOPST" + tpx, SM[:, 416:480], ("top",))
                tap("TOPR" + tpx, sTOPR[:, 0:32], ("rst",))

        def sink_R(tt, ptp_view):
            cp(R[:, :, tt * 128:(tt + 1) * 128], ptp_view, ("pTP",), ("R",))

        top_ln_into(None, None, 32, 0, sink_R)
        tap("R0", R[:, :, :], ("R",))

        def ln_stats(l, nch, ch, murow_o, rstdrow_o, sqbuf, sqkey, wrow_key):
            for c in range(nch):
                tok = slice(c * ch, (c + 1) * ch)
                act(sqbuf, R[:, :, tok], AF.Square, ("R",), (sqkey,))
                for k in range(8):
                    pe_mm(pSTAT[0:1, 0:ch], bONESC, R[:, k, tok],
                          ("R", "onesc"), ("pSTAT",), start=(k == 0), stop=(k == 7))
                for k in range(8):
                    pe_mm(pA[0:1, 0:ch], bONESC, sqbuf[:, k, :],
                          (sqkey, "onesc"), ("pA",), start=(k == 0), stop=(k == 7))
                mu = grow(murow_o + c * ch, ch)
                kmu, krs = wrow_key + "mu%d" % c, wrow_key + "rs%d" % c
                ts(mu, pSTAT[0:1, 0:ch], RINV, None, Op.mult, Op.bypass,
                   ("pSTAT",), (kmu,))
                ts(sVAR1[:, 0:ch], pA[0:1, 0:ch], RINV, EPS, Op.mult, Op.add,
                   ("pA",), ("var1",))
                tt(pSTAT[0:1, 0:ch], mu, mu, Op.mult, (kmu,), ("pSTAT",))
                rsr = grow(rstdrow_o + c * ch, ch)
                tt(rsr, sVAR1[:, 0:ch], pSTAT[0:1, 0:ch], Op.subtract,
                   ("var1", "pSTAT"), (krs,))
            allk = tuple(wrow_key + "rs%d" % i for i in range(nch))
            rall = grow(rstdrow_o, nch * ch)
            act(rall, rall, AF.Ln, allk, allk)
            act(rall, rall, AF.Exp, allk, allk, scale=-0.5)

        def ln_apply(l, tok, ch, murow_o, rstdrow_o, wrow_key, lnw, lnb, xh, lnt,
                     xh_key, row_off=None):
            ro = tok.start if row_off is None else row_off
            ci = ro // ch
            kmu, krs = wrow_key + "mu%d" % ci, wrow_key + "rs%d" % ci
            pe_mm(pSTAT[:, 0:ch], bONESR, grow(murow_o + ro, ch),
                  (kmu, "onesr"), ("pSTAT",))
            rstdb = pSTAT[:, 256:256 + ch] if ch <= 256 else pA[:, 0:ch]
            rkey = "pSTAT" if ch <= 256 else "pA"
            pe_mm(rstdb, bONESR, grow(rstdrow_o + ro, ch),
                  (krs, "onesr"), (rkey,))
            for t in range(8):
                tt(lnt[:, 0, :], R[:, t, tok], pSTAT[:, 0:ch], Op.subtract,
                   ("R", "pSTAT"), ("lnt",))
                tt(lnt[:, 0, :], lnt[:, 0, :], rstdb, Op.mult, ("lnt", rkey), ("lnt",))
                ts(xh[:, t, :], lnt[:, 0, :], lnw[:, t:t + 1], lnb[:, t:t + 1],
                   Op.mult, Op.add, ("lnt", "lnv"), (xh_key,))

        def a1a(l):
            aw = dr["attn_w"][l].rearrange("(k p) (h three c) -> three k p h c",
                                           p=128, three=3, c=64)
            for k in range(8):
                dma("gpsq", vWQ[:, k, :], aw[0, k], ("WQ",), ("WQ",))
                dma("gpsq", vWK[:, k, :], aw[1, k], ("WK",), ("WK",))
                dma("gpsq", vWV[:, k, :], aw[2, k], ("WV",), ("WV",))
            ln_stats(l, NCH1, CH1, MUROW_O, RSTDROW_O, vZ, "Z", "w1")
            if DEBUG and l == 1:
                tap("W1MU", GEN[0:1, MUROW_O:MUROW_O + T],
                    tuple("w1mu%d" % i for i in range(NCH1)))
                tap("W1RS", GEN[0:1, RSTDROW_O:RSTDROW_O + T],
                    tuple("w1rs%d" % i for i in range(NCH1)))

            def P1(c):
                tok = slice(c * CH1, (c + 1) * CH1)
                ln_apply(l, tok, CH1, MUROW_O, RSTDROW_O, "w1",
                         sLN1W[:, l, :], sLN1B[:, l, :], vXH, vLNT, "xh")

            def P2(c):
                p = c % 2
                vQ, vKb, vVb = vQ2P[p], vK2P[p], vV2P[p]
                kq, kk_, kv = "Q%d" % p, "K%d" % p, "V%d" % p
                if c > 0:
                    po = (c - 1) % 2
                    for tnm, tb, hb, hk in (("Q%d" % po, vQ2P[po], vHQ, "HQ"),
                                            ("K%d" % po, vK2P[po], vHK, "HK"),
                                            ("V%d" % po, vV2P[po], vHV, "HV")):
                        cp(hb[:, :, 0:1], tb[:, :, CH1 - 1:CH1], (tnm,), (hk,))
                for tnm, tb, wv, wkey, qoff in ((kq, vQ, vWQ, "WQ", 0),
                                                (kk_, vKb, vWK, "WK", 1),
                                                (kv, vVb, vWV, "WV", 2)):
                    for mt in range(8):
                        bank = (qoff * 8 + mt) % 4
                        pview = pBIG[:, bank * 512: bank * 512 + CH1]
                        bk = "pBG%d" % bank
                        for k in range(8):
                            pe_mm(pview, wv[:, k, mt * 128:(mt + 1) * 128],
                                  vXH[:, k, :], ("xh", wkey), (bk,),
                                  start=(k == 0), stop=(k == 7))
                        act(tb[:, mt, 0:CH1], pview, AF.Identity,
                            (bk, "ab"), (tnm,), bias=sAB[:, l, qoff * 8 + mt:
                                                         qoff * 8 + mt + 1])
                if c == 0:
                    for tnm, tb, hb, hk in (("Q0", vQ2P[0], vHQ, "HQ"),
                                            ("K0", vK2P[0], vHK, "HK"),
                                            ("V0", vV2P[0], vHV, "HV")):
                        cp(hb[:, :, 0:1], tb[:, :, 0:1], (tnm,), (hk,))

            def P3(c):
                p = c % 2
                vQ, vKb, vVb = vQ2P[p], vK2P[p], vV2P[p]
                kq, kk_, kv = "Q%d" % p, "K%d" % p, "V%d" % p
                for s in range(2):
                    sr = slice(s * 4, s * 4 + 4)
                    for tnm, tb, rb, rk in ((kq, vQ, vRQ, "RQ"),
                                            (kk_, vKb, vRK, "RK"),
                                            (kv, vVb, vRV, "RV")):
                        dma("syncq", rb[64:128, :, :], tb[0:64, sr, :],
                            (tnm, rk), (rk,))
                        if s == 0:
                            dma("syncq", rb[0:64, 1:4, :],
                                tb[64:128, 0:3, :], (tnm, rk), (rk,))
                            dma("syncq", rb[0:64, 0:1, :],
                                tb[64:128, 7:8, :], (tnm, rk), (rk,))
                        else:
                            dma("syncq", rb[0:64, 0:4, :],
                                tb[64:128, 3:7, :], (tnm, rk), (rk,))
                    for tb, tnm, hb, hk, db, dk, cb, ck2 in (
                            (vQ, kq, vHQ, "HQ", vDQ, "DQ", vCQ, "CQ"),
                            (vKb, kk_, vHK, "HK", vDK, "DK", vCK, "CK"),
                            (vVb, kv, vHV, "HV", vDV, "DV", vCV, "CV")):
                        tt(db[:, :, 1:CH1], tb[:, sr, 1:CH1], tb[:, sr, 0:CH1 - 1],
                           Op.subtract, (tnm,), (dk,))
                        tt(db[:, :, 0:1], tb[:, sr, 0:1], hb[:, sr, :],
                           Op.subtract, (tnm, hk), (dk,))
                        ct0 = vZDRE[:, :, :]
                        ct1 = vZDIM[:, :, :]
                        act(ct0, tb[:, sr, :], AF.Sin, (tnm,), ("ZDRE",),
                            scale=0.5)
                        act(ct1, ct0, AF.Square, ("ZDRE",), ("ZDIM",),
                            scale=float(np.sqrt(2.0)))
                        ts(cb, ct1, 1.0, -1.0, Op.subtract, Op.mult, ("ZDIM",), (ck2,))
                    qs = (vQ[:, sr, :], vDQ, vRQ, vCQ)
                    kk = (vKb[:, sr, :], vDK, vRK, vCK)
                    vv = (vVb[:, sr, :], vDV, vRV, vCV)
                    qr = (kq, "DQ", "RQ", "CQ")
                    kr = (kk_, "DK", "RK", "CK")
                    vr = (kv, "DV", "RV", "CV")
                    terms = [
                        [(0, 0, 1), (1, 1, -1), (2, 2, -1), (3, 3, -1)],
                        [(0, 1, 1), (1, 0, 1), (2, 3, 1), (3, 2, -1)],
                        [(0, 2, 1), (2, 0, 1), (1, 3, -1), (3, 1, 1)],
                        [(0, 3, 1), (3, 0, 1), (1, 2, 1), (2, 1, -1)],
                    ]
                    zsl = vZ[:, sr, :]
                    for i in range(4):
                        first = True
                        for (a, bq, sgn) in terms[i]:
                            if first:
                                tt(vT0, qs[a], kk[bq], Op.mult,
                                   (qr[a], kr[bq]), ("T0",))
                                first = False
                            else:
                                tt(vT1, qs[a], kk[bq], Op.mult,
                                   (qr[a], kr[bq]), ("T1",))
                                tt(vT0, vT0, vT1,
                                   Op.add if sgn > 0 else Op.subtract,
                                   ("T0", "T1"), ("T0",))
                        tt(vT1, vT0, vv[i], Op.mult, ("T0", vr[i]), ("T1",))
                        if i == 0:
                            cp(zsl, vT1, ("T1",), ("Z",))
                        else:
                            tt(zsl, zsl, vT1, Op.subtract, ("Z", "T1"), ("Z",))

            def P4(c):
                for half in range(2):
                    for t4 in range(4):
                        t = half * 4 + t4
                        pe_mm(pA[:, 0:CH1], bST["c_sd_re"], vZ[:, t, :],
                              ("Z", "c_sd_re"), ("pA",))
                        pe_mm(pB[:, 0:CH1], bST["c_sd_im"], vZ[:, t, :],
                              ("Z", "c_sd_im"), ("pB",))
                        act(vZDRE[:, t4, :], pA[:, 0:CH1], AF.Identity,
                            ("pA",), ("ZDRE",))
                        act(vZDIM[:, t4, :], pB[:, 0:CH1], AF.Identity,
                            ("pB",), ("ZDIM",))
                    hs = slice(half * 4, half * 4 + 4)
                    dma("syncq", bounce[0, 0, c][:, :, hs, :], vZDRE,
                        ("ZDRE",), ("bnc0",))
                    dma("syncq", bounce[0, 1, c][:, :, hs, :], vZDIM,
                        ("ZDIM",), ("bnc0",))
                if DEBUG and l == 0 and c == 0:
                    tap("XH", vXH, ("xh",))
                    tap("Z", vZ, ("Z",))

            P1(0); P2(0)
            for c in range(NCH1):
                if c + 1 < NCH1:
                    P1(c + 1)
                P3(c)
                if c + 1 < NCH1:
                    P2(c + 1)
                P4(c)

        def a1b(l):
            dma("gpsq", vWP, dr["aproj_w"][l].rearrange("(k p) m -> p k m", p=128),
                ("WP",), ("WP",))
            dma("gpsq", GEN[0:1, MODROW_O:MODROW_O + T], dr["c_mod"],
                ("modr",), ("modr",))

            def Q1(c):
                p = c % 2
                tok = slice(c * CH1, (c + 1) * CH1)
                dma("syncq", vFC2P[p], dr["c_filt_c"][:, :, tok],
                    ("FC%d" % p,), ("FC%d" % p,))
                dma("syncq", vFS2P[p], dr["c_filt_s"][:, :, tok],
                    ("FS%d" % p,), ("FS%d" % p,))
                for A in range(8):
                    dma("syncq", vLHRE2P[p][:, A, :],
                        bounce[0, 0, c][:, A].rearrange("sg t tok -> t sg tok"),
                        ("bnc0", "LHRE%d" % p), ("LHRE%d" % p,))
                    dma("syncq", vLHIM2P[p][:, A, :],
                        bounce[0, 1, c][:, A].rearrange("sg t tok -> t sg tok"),
                        ("bnc0", "LHIM%d" % p), ("LHIM%d" % p,))

            def Q2(c):
                p = c % 2
                lre, lim = vLHRE2P[p], vLHIM2P[p]
                kre, kim = "LHRE%d" % p, "LHIM%d" % p
                for ap_ in range(4):
                    asl = slice(2 * ap_, 2 * ap_ + 2)
                    pe_mm(pA[:, :], bST["c_sh_re"], lre[:, asl, :],
                          (kre, "c_sh_re"), ("pA",), start=True, stop=False)
                    pe_mm(pA[:, :], bST["c_sh_imn"], lim[:, asl, :],
                          (kim, "c_sh_imn"), ("pA",), start=False, stop=True)
                    pe_mm(pB[:, :], bST["c_sh_im"], lre[:, asl, :],
                          (kre, "c_sh_im"), ("pB",), start=True, stop=False)
                    pe_mm(pB[:, :], bST["c_sh_re"], lim[:, asl, :],
                          (kim, "c_sh_re"), ("pB",), start=False, stop=True)
                    pav = pA[:, :].rearrange("p (a b) -> p a b", a=2)
                    pbv = pB[:, :].rearrange("p (a b) -> p a b", a=2)
                    tt(vZPRE[:, asl, :], pav, vFC2P[p][:, asl, :], Op.mult,
                       ("pA", "FC%d" % p), ("ZPRE",))
                    tt(vFTMP, pbv, vFS2P[p][:, asl, :], Op.mult,
                       ("pB", "FS%d" % p), ("FT",))
                    tt(vZPRE[:, asl, :], vZPRE[:, asl, :], vFTMP, Op.subtract,
                       ("ZPRE", "FT"), ("ZPRE",))
                    tt(vZPIM[:, asl, :], pav, vFS2P[p][:, asl, :], Op.mult,
                       ("pA", "FS%d" % p), ("ZPIM",))
                    tt(vFTMP, pbv, vFC2P[p][:, asl, :], Op.mult,
                       ("pB", "FC%d" % p), ("FT",))
                    tt(vZPIM[:, asl, :], vZPIM[:, asl, :], vFTMP, Op.add,
                       ("ZPIM", "FT"), ("ZPIM",))
                    hire = pBIG[:, 0:512]
                    hiim = pBIG[:, 512:1024]
                    pe_mm(hire, bST["c_shi_re"], vZPRE[:, asl, :],
                          ("ZPRE", "c_shi_re"), ("pBG0",), start=True, stop=False)
                    pe_mm(hire, bST["c_shi_imn"], vZPIM[:, asl, :],
                          ("ZPIM", "c_shi_imn"), ("pBG0",), start=False, stop=True)
                    pe_mm(hiim, bST["c_shi_im"], vZPRE[:, asl, :],
                          ("ZPRE", "c_shi_im"), ("pBG1",), start=True, stop=False)
                    pe_mm(hiim, bST["c_shi_re"], vZPIM[:, asl, :],
                          ("ZPIM", "c_shi_re"), ("pBG1",), start=False, stop=True)
                    hirev = hire.rearrange("p (a b) -> p a b", a=2)
                    hiimv = hiim.rearrange("p (a b) -> p a b", a=2)
                    act(vYRE[:, asl, :], hirev, AF.Identity, ("pBG0",), ("YRE",))
                    act(vYIM[:, asl, :], hiimv, AF.Identity, ("pBG1",), ("YIM",))

            def Q3w(c):
                dma("syncq", bounce[1, 0, c], vYRE, ("YRE",), ("bnc1",))
                dma("syncq", bounce[1, 1, c], vYIM, ("YIM",), ("bnc1",))

            def Q3r(c):
                p = c % 2
                for A in range(8):
                    dma("syncq", vLHRE2P[p][:, A, :],
                        bounce[1, 0, c][:, A].rearrange("sg t tok -> t sg tok"),
                        ("bnc1", "LHRE%d" % p), ("LHRE%d" % p,))
                    dma("syncq", vLHIM2P[p][:, A, :],
                        bounce[1, 1, c][:, A].rearrange("sg t tok -> t sg tok"),
                        ("bnc1", "LHIM%d" % p), ("LHIM%d" % p,))

            def Q4(c):
                p = c % 2
                tok = slice(c * CH1, (c + 1) * CH1)
                yb = vYOUT2P[p]
                ybk = "YOUT%d" % p
                pe_mm(pSTAT[:, 0:CH1], bONESR, grow(MODROW_O + c * CH1, CH1),
                      ("modr", "onesr"), ("pSTAT",))
                act(vMODB[:, 0, :], pSTAT[:, 0:CH1], AF.Identity,
                    ("pSTAT",), ("MODB",))
                for t in range(8):
                    if t % 2 == 0:
                        pview = pBIG[:, 1536:1536 + CH1]
                        pk2 = "pBG3"
                    else:
                        pview = pSTAT[:, 0:CH1]
                        pk2 = "pSTAT"
                    pe_mm(pview, bST["c_sdi_re"], vLHRE2P[p][:, t, :],
                          ("LHRE%d" % p, "c_sdi_re"), (pk2,),
                          start=True, stop=False)
                    pe_mm(pview, bST["c_sdi_imn"], vLHIM2P[p][:, t, :],
                          ("LHIM%d" % p, "c_sdi_imn"), (pk2,),
                          start=False, stop=True)
                    tt(yb[:, t, :], pview, vMODB[:, 0, :], Op.mult,
                       (pk2, "MODB"), (ybk,))
                for m in range(8):
                    bank = 2 + (m % 2)
                    pview = pBIG[:, bank * 512: bank * 512 + CH1]
                    bk = "pBG%d" % bank
                    for k in range(8):
                        pe_mm(pview, vWP[:, k, m * 128:(m + 1) * 128], yb[:, k, :],
                              (ybk, "WP"), (bk,), start=(k == 0), stop=(k == 7))
                    stt(R[:, m, tok], pview, sPB[:, l, m:m + 1], R[:, m, tok],
                        Op.add, Op.add, (bk, "pb", "R"), ("R",))
                if DEBUG and l == 0 and c == 0:
                    tap("LHRE", vLHRE2P[0], ("LHRE0",))
                    tap("ZPRE", vZPRE, ("ZPRE",))
                    tap("YOUT", vYOUT2P[0], ("YOUT0",))

            Q1(0)
            if NCH1 > 1:
                Q1(1)
            for c in range(NCH1):
                Q2(c)
                Q3w(c)
                if c + 2 < NCH1:
                    Q1(c + 2)
            for c in range(NCH1):
                Q3r(c)
                Q4(c)

        def ln_stats2(l, th):
            for c2 in range(NCH2 // 2):
                tok = slice(th * THALF + c2 * CH2, th * THALF + (c2 + 1) * CH2)
                sqb = vFCW[:, 0:8, 0:CH2]
                act(sqb, R[:, :, tok], AF.Square, ("R", "FCW"), ("FCW",))
                for k in range(8):
                    pe_mm(pSTAT[0:1, 0:CH2], bONESC, R[:, k, tok],
                          ("R", "onesc"), ("pSTAT",), start=(k == 0), stop=(k == 7))
                for k in range(8):
                    pe_mm(pA[0:1, 0:CH2], bONESC, sqb[:, k, :],
                          ("FCW", "onesc"), ("pA",), start=(k == 0), stop=(k == 7))
                mu = grow(MUROW2_O + c2 * CH2, CH2)
                kmu, krs = "w2mu%d" % c2, "w2rs%d" % c2
                ts(mu, pSTAT[0:1, 0:CH2], RINV, None, Op.mult, Op.bypass,
                   ("pSTAT",), (kmu,))
                ts(sVAR1[:, 0:CH2], pA[0:1, 0:CH2], RINV, EPS, Op.mult, Op.add,
                   ("pA",), ("var1",))
                tt(pSTAT[0:1, 0:CH2], mu, mu, Op.mult, (kmu,), ("pSTAT",))
                rsr = grow(RSTDROW2_O + c2 * CH2, CH2)
                tt(rsr, sVAR1[:, 0:CH2], pSTAT[0:1, 0:CH2], Op.subtract,
                   ("var1", "pSTAT"), (krs,))
            allk = tuple("w2rs%d" % i for i in range(NCH2 // 2))
            rall = grow(RSTDROW2_O, THALF)
            act(rall, rall, AF.Ln, allk, allk)
            act(rall, rall, AF.Exp, allk, allk, scale=-0.5)

        def a2(l):
            for th in range(2):
                ln_stats2(l, th)
                if DEBUG and l == 0 and th == 0:
                    tap("W2MU", GEN[0:1, MUROW2_O:MUROW2_O + THALF],
                        tuple("w2mu%d" % i for i in range(4)))
                    tap("W2RS", GEN[0:1, RSTDROW2_O:RSTDROW2_O + THALF],
                        tuple("w2rs%d" % i for i in range(4)))
                for cc in range(NCH2 // 2):
                    tok = slice(th * THALF + cc * CH2, th * THALF + (cc + 1) * CH2)
                    xsl = slice(cc * CH2, (cc + 1) * CH2)
                    ln_apply(l, tok, CH2, MUROW2_O, RSTDROW2_O, "w2",
                             sLN2W[:, l, :], sLN2B[:, l, :],
                             vXH2H[:, :, xsl], vLNT2, "xh2", row_off=xsl.start)
                for hh in range(2):
                    dma("gpsq", vFCW,
                        dr["fc_w"][l][:, hh * 2048:(hh + 1) * 2048]
                        .rearrange("(k p) m -> p k m", p=128), ("FCW",), ("FCW",))
                    dma("gpsq", vMPW,
                        dr["mproj_w"][l][hh * 2048:(hh + 1) * 2048, :]
                        .rearrange("(k p) m -> p k m", p=128), ("MPW",), ("MPW",))
                    for cc in range(NCH2 // 2):
                        tok = slice(th * THALF + cc * CH2, th * THALF + (cc + 1) * CH2)
                        xsl = slice(cc * CH2, (cc + 1) * CH2)
                        for hm in range(16):
                            pv = (pA if hm % 2 == 0 else pB)[:, :]
                            pk = "pA" if hm % 2 == 0 else "pB"
                            for k in range(8):
                                pe_mm(pv, vFCW[:, k, hm * 128:(hm + 1) * 128],
                                      vXH2H[:, k, xsl], ("xh2", "FCW"), (pk,),
                                      start=(k == 0), stop=(k == 7))
                            act(vH1[:, hm, :], pv, AF.Gelu, (pk, "fb"),
                                ("H1",) + tuple("w2mu%d" % i for i in range(4))
                                + tuple("w2rs%d" % i for i in range(4)),
                                bias=sFB[:, l, hh * 16 + hm: hh * 16 + hm + 1])
                        for mh in range(2):
                            for m4 in range(4):
                                m = mh * 4 + m4
                                pview = pBIG[:, m4 * 512: (m4 + 1) * 512]
                                bk = "pBG%d" % m4
                                for k in range(16):
                                    pe_mm(pview, vMPW[:, k, m * 128:(m + 1) * 128],
                                          vH1[:, k, :], ("H1", "MPW"), (bk,),
                                          start=(k == 0), stop=(k == 15))
                                if hh == 0:
                                    stt(R[:, m, tok], pview, sMB[:, l, m:m + 1],
                                        R[:, m, tok], Op.add, Op.add,
                                        (bk, "mb", "R"), ("R",))
                                else:
                                    tt(R[:, m, tok], R[:, m, tok], pview, Op.add,
                                       ("R", bk), ("R",))

        def fence():
            S.fence(lambda e: e.memset(SMB[0:1, 2120:2121], 0.0))

        for l in range(L):
            a1a(l)
            fence()
            a1b(l)
            tap("Rattn%d" % l, R[:, :, :], ("R",))
            fence()
            a2(l)
            tap("Rlayer%d" % l, R[:, :, :], ("R",))
            fence()

        def sink_xnh(tt, ptp_view, th):
            cp(vXNH[:, :, tt * 128:(tt + 1) * 128], ptp_view, ("pTP",), ("XNH",))

        outk = []
        for th in range(2):
            tok0 = th * THALF
            top_ln_into(None, None, 16, tok0, lambda tt, v: sink_xnh(tt, v, th),
                        tpx="B%d" % th)
            for cc in range(THALF // 512):
                for k in range(8):
                    pe_mm(pSTAT[0:8, :], bROUT[:, k, :],
                          vXNH[:, k, cc * 512:(cc + 1) * 512],
                          ("XNH", "rout"), ("pSTAT",), start=(k == 0), stop=(k == 7))
                act(sLGT, pSTAT[0:8, :], AF.Identity, ("pSTAT",), ("lgt",))
                for bb in range(4):
                    g = cc * 4 + bb
                    pe_tr(pA[:, 0:8], sLGT[:, bb * 128:(bb + 1) * 128], sIDF[0:8, 0:8],
                          ("lgt", "idf"), ("pA",))
                    cp(sLTOK[:, g, :], pA[:, 0:8], ("pA",), ("ltok",))
            act(SM[:, 496:624], SM[:, 496:624], AF.Exp, ("ltok",), ("ltok",))
            red(sSUMS, sLTOK, Op.add, ("ltok",), ("sums",))
            S.op("dve", lambda e: e.reciprocal(sSUMS, sSUMS), ("sums",), ("sums",))
            for g in range(16):
                ts(sCOEFP[:, g, :], sLTOK[:, g, :], sSUMS[:, g:g + 1], None,
                   Op.mult, Op.bypass, ("ltok", "sums"), ("cfp",))
            red(sM1, sCOEFP, Op.max, ("cfp",), ("m1",))
            for g in range(16):
                ts(sSUP, sCOEFP[:, g, :], sM1[:, g:g + 1], -1e9, Op.is_ge, Op.mult,
                   ("cfp", "m1"), ("sup",))
                tt(sSUP, sCOEFP[:, g, :], sSUP, Op.add, ("cfp", "sup"), ("sup",))
                red(sM2[:, g:g + 1], sSUP, Op.max, ("sup",), ("m2",))
            for g in range(16):
                ts(sSUP, sCOEFP[:, g, :], sM2[:, g:g + 1], None, Op.is_ge, Op.bypass,
                   ("cfp", "m2"), ("sup",))
                tt(vCOEFT[:, g, :], sCOEFP[:, g, :], sSUP, Op.mult,
                   ("cfp", "sup"), ("W2",))
            for g in range(16):
                pe_tr(pTP[0:8, 0:128], vCOEFT[:, g, :], bIDB, ("W2", "idb"), ("pTP",))
                cp(vCOEFROW[:, g * 128:(g + 1) * 128], pTP[0:8, 0:128],
                   ("pTP",), ("crow",))
            if th == 0:
                tap("XNH", vXNH, ("XNH",))
                tap("COEFROW", GEN[0:8, 57344:57344 + 2048], ("crow",))
            dma("gpsq", vSGW, dr["sg"].rearrange("(k p) m -> p k m", p=128),
                ("EW0",), ("EW0",))
            dma("gpsq", vSUW, dr["su"].rearrange("(k p) m -> p k m", p=128),
                ("EW0", "EW1"), ("EW0", "EW1"))
            dma("gpsq", vSDW, dr["sd"].rearrange("(k p) m -> p k m", p=128),
                ("EW1",), ("EW1",))
            nchb = THALF // CHB
            for cc in range(nchb):
                tok = slice(tok0 + cc * CHB, tok0 + (cc + 1) * CHB)
                xsl = slice(cc * CHB, (cc + 1) * CHB)
                for mh in range(2):
                    for j in range(4):
                        m = mh * 4 + j
                        pv = (pA if j % 2 == 0 else pB)[:, (j // 2) * 256:
                                                        (j // 2) * 256 + CHB]
                        pk = "pA" if j % 2 == 0 else "pB"
                        for k in range(8):
                            pe_mm(pv, vSGW[:, k, m * 128:(m + 1) * 128],
                                  vXNH[:, k, xsl], ("XNH", "EW0"), (pk,),
                                  start=(k == 0), stop=(k == 7))
                        act(vSGSH[:, j, :], pv, AF.Silu, (pk,), ("W1",))
                        for k in range(8):
                            pe_mm(pv, vSUW[:, k, m * 128:(m + 1) * 128],
                                  vXNH[:, k, xsl], ("XNH", "EW0", "EW1"), (pk,),
                                  start=(k == 0), stop=(k == 7))
                        tt(vHUSH[:, m, :], pv, vSGSH[:, j, :], Op.mult,
                           (pk, "W1"), ("W2", "W3"))
                for m in range(8):
                    pview = pBIG[:, m * 256: m * 256 + CHB]
                    bk = "pBG%d" % (m // 2)
                    for k in range(8):
                        pe_mm(pview, vSDW[:, k, m * 128:(m + 1) * 128],
                              vHUSH[:, k, :], ("W2", "W3", "EW1"), (bk,),
                              start=(k == 0), stop=(k == 7))
                act(vACC[:, :, xsl],
                    pBIG[:, :].rearrange("p (a b) -> p a b", a=8), AF.Identity,
                    ("pBG0", "pBG1", "pBG2", "pBG3"), ("ACC",))
            for e_i in range(NEXP):
                o = EW_O[e_i % 2]
                vGW = gv(o, 8, 512); vUW = gv(o + 4096, 8, 512)
                vDW = gv(o + 8192, 4, 1024)
                wk = "EW%d" % (e_i % 2)
                dma("gpsq", vGW, dr["eg"][e_i].rearrange("(k p) m -> p k m", p=128),
                    (wk,), (wk,))
                dma("gpsq", vUW, dr["eu"][e_i].rearrange("(k p) m -> p k m", p=128),
                    (wk,), (wk,))
                dma("gpsq", vDW, dr["ed"][e_i].rearrange("(k p) m -> p k m", p=128),
                    (wk,), (wk,))
                for cc in range(nchb // 2):
                    xsl = slice(cc * 512, (cc + 1) * 512)
                    dma("syncq", vCOEFR0B, vCOEFROW[e_i:e_i + 1, xsl],
                        ("crow", "W3"), ("W3",))
                    pe_mm(pSTAT[:, 0:512], bONESR, vCOEFR0B,
                          ("W3", "onesr"), ("pSTAT",))
                    for j in range(4):
                        pv = pBIG[:, j * 512:(j + 1) * 512]
                        pk = "pBG%d" % j
                        for k in range(8):
                            pe_mm(pv, vGW[:, k, j * 128:(j + 1) * 128],
                                  vXNH[:, k, xsl], ("XNH", wk), (pk,),
                                  start=(k == 0), stop=(k == 7))
                        act(vSGE[:, j % 2, :], pv, AF.Silu, (pk,), ("W1%d" % (j % 2),))
                        for k in range(8):
                            pe_mm(pv, vUW[:, k, j * 128:(j + 1) * 128],
                                  vXNH[:, k, xsl], ("XNH", wk), (pk,),
                                  start=(k == 0), stop=(k == 7))
                        tt(vHUE[:, j, :], pv, vSGE[:, j % 2, :], Op.mult,
                           (pk, "W1%d" % (j % 2)), ("W2",))
                        tt(vHUE[:, j, :], vHUE[:, j, :], pSTAT[:, 0:512], Op.mult,
                           ("W2", "pSTAT"), ("W2",))
                    for mq in range(4):
                        for m2 in range(2):
                            m = mq * 2 + m2
                            pview = (pA if m2 == 0 else pB)[:, :]
                            bk = "pA" if m2 == 0 else "pB"
                            for k in range(4):
                                pe_mm(pview, vDW[:, k, m * 128:(m + 1) * 128],
                                      vHUE[:, k, :], ("W2", wk), (bk,),
                                      start=(k == 0), stop=(k == 3))
                            tt(vACC[:, m, xsl], vACC[:, m, xsl], pview, Op.add,
                               ("ACC", bk), ("ACC",))
            if th == 0:
                tap("ACC", vACC, ("ACC",))
            for tt_i in range(THALF // 128):
                xt = GENF[:, 0:1024]
                ot = GENF[:, 1024:2048]
                tsl = slice(tt_i * 128, (tt_i + 1) * 128)
                gsl = slice(tok0 + tt_i * 128, tok0 + (tt_i + 1) * 128)
                dma("syncq", xt, dr["x"][gsl, :], ("xt0", "xt1", "ot"), ("xt0",))
                for fc in range(8):
                    act(vFSIL[:, 0, :], vACC[:, fc, tsl], AF.Silu, ("ACC",), ("W2",))
                    tt(vFPROD[:, 0, :], vFSIL[:, 0, :], R[:, fc, gsl], Op.mult,
                       ("W2", "R"), ("W2",))
                    pe_tr(pTP[:, fc * 128:(fc + 1) * 128], vFPROD[:, 0, :], bIDB,
                          ("W2", "idb"), ("pTP",))
                tt(ot, pTP[:, 0:1024], xt, Op.add, ("pTP", "xt0"), ("ot", "xt1"))
                ok = "out%d_%d" % (th, tt_i)
                outk.append(ok)
                dma("syncq", out_dr[gsl, :], ot, ("ot",), (ok,))
            fence()

        S.op("dve", lambda e: e.memset(SMB[0:1, 2121:2122], 0.0), tuple(outk), ())

        emit_lists, final_counts = S.finalize()

        def run_engine(e, name):
            for wl, o in emit_lists[name]:
                for (se, v) in wl:
                    e.wait_ge(sems[se], v)
                ins = o.emit(e)
                if o.need_inc:
                    ins.then_inc(sems[o.sem], o.inc)

        @block.tensor
        def _(e):
            run_engine(e, "pe")

        @block.vector
        def _(e):
            run_engine(e, "dve")

        @block.scalar
        def _(e):
            run_engine(e, "act")

        @block.sync
        def _(e):
            run_engine(e, "syncq")

        @block.gpsimd
        def _(e):
            run_engine(e, "gpsq")

    _PROG = nc
    return nc


def kernel(**inputs):
    cs = host_constants()
    nc = build_program()
    x = np.ascontiguousarray(np.asarray(inputs["L0_attn_x"]))
    base = {
        "ln1_w": inputs["L0_attn_ln1_w"], "ln1_b": inputs["L0_attn_ln1_b"],
        "attn_w": inputs["L0_attn_attn_w"], "attn_b": inputs["L0_attn_attn_b"],
        "aproj_w": inputs["L0_attn_aproj_w"], "aproj_b": inputs["L0_attn_aproj_b"],
        "ln2_w": inputs["L0_attn_ln2_w"], "ln2_b": inputs["L0_attn_ln2_b"],
        "fc_w": inputs["L0_attn_fc_w"], "fc_b": inputs["L0_attn_fc_b"],
        "mproj_w": inputs["L0_attn_mproj_w"], "mproj_b": inputs["L0_attn_mproj_b"],
        "router": inputs["L0_ffn_router"],
        "eg": inputs["L0_ffn_experts_gate"], "eu": inputs["L0_ffn_experts_up"],
        "ed": inputs["L0_ffn_experts_down"],
        "sg": inputs["L0_ffn_shared_gate"], "su": inputs["L0_ffn_shared_up"],
        "sd": inputs["L0_ffn_shared_down"],
    }
    base = {k: np.ascontiguousarray(np.asarray(v, dtype=np.float32))
            for k, v in base.items()}
    def fmaj(v):
        return np.asarray(v, np.float32).reshape(8, 128).T
    cols = []
    ab = np.asarray(base["attn_b"], np.float32).reshape(L, 16, 3, 64)
    cols.append(np.concatenate(
        [np.concatenate([ab[l, :, q, :].reshape(8, 128).T for q in range(3)], 1)
         for l in range(L)], 1))
    cols.append(np.concatenate([fmaj(base["aproj_b"][l]) for l in range(L)], 1))
    fb = np.asarray(base["fc_b"], np.float32).reshape(L, 32, 128)
    cols.append(np.concatenate([fb[l].T for l in range(L)], 1))
    cols.append(np.concatenate([fmaj(base["mproj_b"][l]) for l in range(L)], 1))
    for nmv in ("ln1_w", "ln1_b", "ln2_w", "ln2_b"):
        cols.append(np.concatenate([fmaj(base[nmv][l]) for l in range(L)], 1))
    base["biasf"] = np.ascontiguousarray(np.concatenate(cols, axis=1))
    assert base["biasf"].shape == (128, 416)
    base["routf"] = np.ascontiguousarray(
        np.asarray(base["router"], np.float32).reshape(8, 128, 8)
        .transpose(1, 0, 2).reshape(128, 64))
    base.update(cs)
    in_maps = []
    for c in range(NCORE):
        m = dict(base)
        m["x"] = np.ascontiguousarray(x[c])
        in_maps.append(m)
    res = run_bass_kernel_spmd(nc, in_maps, core_ids=list(range(NCORE)))
    out = np.stack([res.results[c]["out"] for c in range(NCORE)], axis=0)
    return out.astype(np.float32)


# version 16
# speedup vs baseline: 1.1726x; 1.1726x over previous
import numpy as np
import ml_dtypes
import concourse.bass as bass
import concourse.mybir as mybir
from concourse.bass_utils import run_bass_kernel_spmd
from concourse.alu_op_type import AluOpType as Op

F32 = mybir.dt.float32
BF16 = mybir.dt.bfloat16
AF = mybir.ActivationFunctionType
BF = ml_dtypes.bfloat16

T = 4096
E = 1024
H, HD = 16, 64
L = 4
HIDDEN = 4096
NEXP, DEXP, DSH = 8, 512, 1024
EPS = 1e-5
NCORE = 8

CH1 = 256
NCH1 = T // CH1
CH2 = 512
NCH2 = T // CH2
CHB = 256
THALF = T // 2
RINV = 1.0 / E

_CONSTS = None


def host_constants():
    global _CONSTS
    if _CONSTS is not None:
        return _CONSTS
    n64 = np.arange(64)
    W64 = np.exp(-2j * np.pi * np.outer(n64, n64) / 64)
    n16 = np.arange(16)
    W16 = np.exp(-2j * np.pi * np.outer(n16, n16) / 16)
    W64i = np.conj(W64) / 64.0
    W16i = np.conj(W16) / 16.0

    Sd = np.zeros((128, 128), np.complex128)
    for hpar in range(2):
        for d in range(64):
            for kd in range(64):
                Sd[hpar * 64 + d, ((kd >> 3) * 2 + hpar) * 8 + (kd & 7)] = W64[d, kd]
    Sh = np.zeros((128, 128), np.complex128)
    Shi = np.zeros((128, 128), np.complex128)
    for Bq in range(8):
        for kh in range(16):
            for tau in range(8):
                for hpar in range(2):
                    Sh[tau * 16 + Bq * 2 + hpar, kh * 8 + Bq] = W16[2 * tau + hpar, kh]
                    Shi[kh * 8 + Bq, (Bq * 2 + hpar) * 8 + tau] = W16i[kh, 2 * tau + hpar]
    Sdi = np.zeros((128, 128), np.complex128)
    for Bq in range(8):
        for A in range(8):
            for hpar in range(2):
                for d in range(64):
                    Sdi[A * 16 + Bq * 2 + hpar, hpar * 64 + d] = W64i[8 * Bq + A, d]

    ks = np.fft.fftfreq(T).astype(np.float64)[:, None, None]
    khf = np.fft.fftfreq(H).astype(np.float64)[None, :, None]
    kdf = np.fft.fftfreq(HD).astype(np.float64)[None, None, :]
    km = np.sqrt(ks ** 2 + khf ** 2 + kdf ** 2 + 1e-10)
    Filt = np.exp(1j * np.arctan(np.log(km + 1e-10)))
    C = np.zeros((128, 8, T), np.float32)
    Sg = np.zeros((128, 8, T), np.float32)
    for Bq in range(8):
        for kh in range(16):
            p = kh * 8 + Bq
            for a in range(8):
                C[p, a, :] = Filt[:, kh, 8 * Bq + a].real.astype(np.float32)
                Sg[p, a, :] = Filt[:, kh, 8 * Bq + a].imag.astype(np.float32)

    lam = np.arange(T, dtype=np.float32) / T
    tt = np.arange(T, dtype=np.float32) * 0.1
    mod = (1.0 + 0.1 * np.sin(tt + lam) * np.cos(tt - 2.0 * lam + 0.5 * lam ** 2))

    cs = {
        "c_sd_re": Sd.real, "c_sd_im": Sd.imag,
        "c_sh_re": Sh.real, "c_sh_im": Sh.imag, "c_sh_imn": -Sh.imag,
        "c_shi_re": Shi.real, "c_shi_im": Shi.imag, "c_shi_imn": -Shi.imag,
        "c_sdi_re": Sdi.real, "c_sdi_imn": -Sdi.imag,
    }
    cs = {k: v.astype(BF) for k, v in cs.items()}
    cs["c_filt_c"] = C.astype(BF)
    cs["c_filt_s"] = Sg.astype(BF)
    cs["c_mod"] = mod.astype(np.float32).reshape(1, T)
    cs["c_id_bf"] = np.eye(128, dtype=BF)
    cs["c_id_f32"] = np.eye(128, dtype=np.float32)
    _CONSTS = cs
    return cs


class _OpRec:
    __slots__ = ("engine", "emit", "deps", "rawdeps", "inc", "need_inc", "cum",
                 "sem", "prev_slot", "dense", "eidx")

    def __init__(self, engine, emit, deps, rawdeps, inc, dense, eidx):
        self.engine = engine
        self.emit = emit
        self.deps = deps
        self.rawdeps = rawdeps
        self.inc = inc
        self.need_inc = False
        self.cum = 0
        self.sem = engine
        self.prev_slot = None
        self.dense = dense
        self.eidx = eidx


def _dense_ap(ap):
    try:
        from concourse.bass_primitives import MemorySpace
        if ap.space == MemorySpace.PSUM:
            return False
    except Exception:
        return False
    try:
        a = ap.opt().ap
    except Exception:
        try:
            a = ap.ap
        except Exception:
            return False
    dims = [d for d in a[1:] if d[1] > 1]
    dims.sort(key=lambda d: -d[0])
    run = 1
    for step, count in reversed(dims):
        if step != run:
            return False
        run = step * count
    return True


ENGS = ("pe", "dve", "act", "syncq", "gpsq")
SYNC_SAME = ("dve",)
NSLOT = 8
DMASEMS = tuple("%s%d" % (q, i) for q in ("syncq", "gpsq") for i in range(NSLOT))
ALLSEMS = ENGS + DMASEMS


class Sched:
    def __init__(self):
        self.ops = []
        self.last_write = {}
        self.readers = {}
        self.ecount = {}

    def __init_extra(self):
        pass

    def op(self, engine, emit, reads=(), writes=(), dma=False, aps=()):
        reads = tuple(reads) + ("EP",)
        rawdeps = set()
        wdeps = set()
        for k in reads:
            w = self.last_write.get(k)
            if w is not None:
                rawdeps.add(w)
        for k in writes:
            w = self.last_write.get(k)
            if w is not None:
                wdeps.add(w)
            rd = self.readers.get(k)
            if rd:
                wdeps.update(rd)
        rawdeps -= wdeps
        i = len(self.ops)
        dense = all(_dense_ap(a) for a in aps) if aps else False
        eidx = self.ecount.get(engine, 0)
        self.ecount[engine] = eidx + 1
        self.ops.append(_OpRec(engine, emit, rawdeps | wdeps, rawdeps,
                               16 if dma else 1, dense, eidx))
        for k in reads:
            self.readers.setdefault(k, []).append(i)
        for k in writes:
            self.last_write[k] = i
            self.readers[k] = []
        return i

    def fence(self, emit):
        self.op("dve", emit, reads=(), writes=("EP",))

    def finalize(self):
        slot_rr = {"syncq": 0, "gpsq": 0}
        slot_last = {s: None for s in DMASEMS}
        for i, o in enumerate(self.ops):
            if o.inc == 16:
                s = "%s%d" % (o.engine, slot_rr[o.engine])
                slot_rr[o.engine] = (slot_rr[o.engine] + 1) % NSLOT
                o.sem = s
                o.prev_slot = slot_last[s]
                slot_last[s] = i
        def skip_same(dep, o, d):
            if dep.engine != o.engine or dep.inc != 1:
                return False
            if dep.engine not in SYNC_SAME:
                return True
            if d not in o.rawdeps:
                return False
            return o.eidx - dep.eidx >= 2

        for o in self.ops:
            if o.inc == 16:
                o.need_inc = True
            for d in o.deps:
                dep = self.ops[d]
                if skip_same(dep, o, d):
                    continue
                dep.need_inc = True
        cum = {e: 0 for e in ALLSEMS}
        for o in self.ops:
            if o.need_inc:
                cum[o.sem] += o.inc
                o.cum = cum[o.sem]
        emit_lists = {e: [] for e in ENGS}
        waited = {e: {s: 0 for s in ALLSEMS} for e in ENGS}
        for o in self.ops:
            waits = {}
            for d in o.deps:
                dep = self.ops[d]
                if skip_same(dep, o, d):
                    continue
                if waits.get(dep.sem, 0) < dep.cum:
                    waits[dep.sem] = dep.cum
            if o.prev_slot is not None:
                dep = self.ops[o.prev_slot]
                if waits.get(dep.sem, 0) < dep.cum:
                    waits[dep.sem] = dep.cum
            wl = []
            for se, v in waits.items():
                if waited[o.engine][se] < v:
                    waited[o.engine][se] = v
                    wl.append((se, v))
            emit_lists[o.engine].append((wl, o))
        return emit_lists, cum


_PROG = None
DEBUG = False


def build_program():
    global _PROG
    if _PROG is not None and not DEBUG:
        return _PROG
    nc = bass.Bass("TRN2", target_bir_lowering=False)
    S = Sched()
    cs = host_constants()

    dr = {}
    dr["x"] = nc.dram_tensor("x", [T, E], F32, kind="ExternalInput").ap()
    out_dr = nc.dram_tensor("out", [T, E], F32, kind="ExternalOutput").ap()
    for nm, shp in [
        ("ln1_w", [L, E]), ("ln1_b", [L, E]),
        ("attn_w", [L, E, 3 * E]), ("attn_b", [L, 3 * E]),
        ("aproj_w", [L, E, E]), ("aproj_b", [L, E]),
        ("ln2_w", [L, E]), ("ln2_b", [L, E]),
        ("fc_w", [L, E, HIDDEN]), ("fc_b", [L, HIDDEN]),
        ("mproj_w", [L, HIDDEN, E]), ("mproj_b", [L, E]),
        ("router", [E, NEXP]),
        ("eg", [NEXP, E, DEXP]), ("eu", [NEXP, E, DEXP]), ("ed", [NEXP, DEXP, E]),
        ("sg", [E, DSH]), ("su", [E, DSH]), ("sd", [DSH, E]),
        ("biasf", [128, 416]), ("routf", [128, 64]),
    ]:
        dr[nm] = nc.dram_tensor(nm, shp, F32, kind="ExternalInput").ap()
    for nm, arr in cs.items():
        dt = BF16 if arr.dtype == BF else F32
        dr[nm] = nc.dram_tensor(nm, list(arr.shape), dt, kind="ExternalInput").ap()
    bounce = nc.dram_tensor("bounce", [2, 2, NCH1, 16, 8, 8, CH1], BF16,
                            kind="Internal").ap()

    from contextlib import ExitStack
    es = ExitStack()
    with es:
        R = es.enter_context(nc.sbuf_tensor("R", [128, 8, T], BF16))
        GEN = es.enter_context(nc.sbuf_tensor("GEN", [128, 62464], BF16))
        GENF = es.enter_context(nc.sbuf_tensor("GENF", [128, 2048], F32))
        SM = es.enter_context(nc.sbuf_tensor("SM", [128, 2048], F32))
        SMB = es.enter_context(nc.sbuf_tensor("SMB", [128, 2664], BF16))
        pSTAT = es.enter_context(nc.psum_tensor("pSTAT", [128, 512], F32))
        pA = es.enter_context(nc.psum_tensor("pA", [128, 512], F32))
        pB = es.enter_context(nc.psum_tensor("pB", [128, 512], F32))
        pTP = es.enter_context(nc.psum_tensor("pTP", [128, 1024], BF16))
        pBIG = es.enter_context(nc.psum_tensor("pBIG", [128, 2048], F32))
        sems = {e: es.enter_context(nc.semaphore("sem_" + e)) for e in ALLSEMS}
        block = es.enter_context(nc.Block())

        def gv(off, *dims):
            n = int(np.prod(dims))
            apx = GEN[:, off:off + n]
            if len(dims) > 1:
                apx = apx.rearrange("p (a b) -> p a b", a=dims[0], b=dims[1])
            return apx

        def grow(off, n):
            return GEN[0:1, off:off + n]

        vWQ = gv(0, 8, 1024); vWK = gv(8192, 8, 1024); vWV = gv(16384, 8, 1024)
        vXH = gv(24576, 8, CH1)
        vQ2P = [gv(26624, 8, CH1), gv(28672, 8, CH1)]
        vK2P = [gv(30720, 8, CH1), gv(32768, 8, CH1)]
        vV2P = [gv(34816, 8, CH1), gv(36864, 8, CH1)]
        vRQ = gv(38912, 4, CH1); vRK = gv(39936, 4, CH1); vRV = gv(40960, 4, CH1)
        vDQ = gv(41984, 4, CH1); vDK = gv(43008, 4, CH1); vDV = gv(44032, 4, CH1)
        vCQ = gv(45056, 4, CH1); vCK = gv(46080, 4, CH1); vCV = gv(47104, 4, CH1)
        vT0 = gv(48128, 4, CH1); vT1 = gv(49152, 4, CH1)
        vZ = gv(50176, 8, CH1)
        vZDRE = gv(52224, 4, CH1); vZDIM = gv(53248, 4, CH1)
        MUROW_O, RSTDROW_O = 54272, 58368
        vLNT = vT0[:, 0:1, :]
        vHQ = SMB[:, 2128:2136].rearrange("p (a b) -> p a b", a=8)
        vHK = SMB[:, 2136:2144].rearrange("p (a b) -> p a b", a=8)
        vHV = SMB[:, 2144:2152].rearrange("p (a b) -> p a b", a=8)
        vWP = gv(0, 8, 1024)
        vFC2P = [gv(8192, 8, CH1), gv(10240, 8, CH1)]
        vFS2P = [gv(12288, 8, CH1), gv(14336, 8, CH1)]
        vLHRE2P = [gv(16384, 8, CH1), gv(18432, 8, CH1)]
        vLHIM2P = [gv(20480, 8, CH1), gv(22528, 8, CH1)]
        vZPRE = gv(24576, 8, CH1); vZPIM = gv(26624, 8, CH1)
        vYRE = gv(28672, 8, CH1); vYIM = gv(30720, 8, CH1)
        vYOUT2P = [gv(32768, 8, CH1), gv(34816, 8, CH1)]
        vMODB = gv(36864, 1, CH1)
        vFTMP = gv(37120, 2, CH1)
        MODROW_O = 39424
        vFCW = gv(0, 8, 2048); vMPW = gv(16384, 16, 1024)
        vXH2H = gv(32768, 8, THALF)
        MUROW2_O, RSTDROW2_O = 49152, 53248
        vH1 = gv(49152, 16, CH2)
        vLNT2 = gv(57344, 1, CH2)
        vXNH = gv(0, 8, THALF); vACC = gv(16384, 8, THALF)
        EW_O = [32768, 45056]
        vSGW = gv(32768, 8, 1024); vSUW = gv(40960, 8, 1024); vSDW = gv(49152, 8, 1024)
        COEFROW_O = 57344
        vCOEFROW = GEN[0:8, 57344:57344 + 2048]
        WORK_O = 59392
        vXNT = gv(59392, 1, 1024)
        vCOEFT = GEN[:, 60416:60416 + 128].rearrange("p (g c) -> p g c", g=16)
        vFSIL = gv(60544, 1, 128)
        vFPROD = gv(60672, 1, 128)
        vSGE = gv(59392, 2, 512); vHUE = gv(60416, 4, 512)
        vCOEFR0B = SMB[0:1, 2152:2664]
        vSGSH = gv(59392, 4, CHB)
        vHUSH = gv(60416, 8, CHB)

        sAB = SM[:, 0:96].rearrange("p (l m) -> p l m", l=L)
        sPB = SM[:, 96:128].rearrange("p (l m) -> p l m", l=L)
        sFB = SM[:, 128:256].rearrange("p (l m) -> p l m", l=L)
        sMB = SM[:, 256:288].rearrange("p (l m) -> p l m", l=L)
        sLN1W = SM[:, 288:320].rearrange("p (l m) -> p l m", l=L)
        sLN1B = SM[:, 320:352].rearrange("p (l m) -> p l m", l=L)
        sLN2W = SM[:, 352:384].rearrange("p (l m) -> p l m", l=L)
        sLN2B = SM[:, 384:416].rearrange("p (l m) -> p l m", l=L)
        sTOP = SM[:, 416:480].rearrange("p (t two) -> p t two", t=32)
        sBN = SM[:, 480:492]
        sLTOK = SM[:, 496:624].rearrange("p (g c) -> p g c", g=16)
        sSUMS = SM[:, 624:640]
        sM1 = SM[:, 640:656]
        sM2 = SM[:, 656:672]
        sCOEFP = SM[:, 672:800].rearrange("p (g c) -> p g c", g=16)
        sSUP = SM[:, 800:808]
        sLGT = SM[0:8, 808:1320]
        sIDF = SM[:, 1320:1448]
        sVAR1 = SM[0:1, 1448:1960]
        sTOPR = SM[:, 1960:1992]

        smb_names = ["c_sd_re", "c_sd_im", "c_sh_re", "c_sh_im", "c_sh_imn",
                     "c_shi_re", "c_shi_im", "c_shi_imn", "c_sdi_re", "c_sdi_imn"]
        bST = {nm: SMB[:, i * 128:(i + 1) * 128] for i, nm in enumerate(smb_names)}
        bIDB = SMB[:, 1280:1408]
        bONESR = SMB[0:1, 1408:1536]
        bONESC = SMB[:, 1536:1537]
        bROUT = SMB[:, 1540:1604].rearrange("p (k m) -> p k m", k=8)
        bNEGH = SMB[0:1, 1608:2120]

        def dma(q, outap, inap, reads, writes, **kw):
            S.op(q, lambda e, o=outap, i=inap, k=kw: e.dma_start(out=o, in_=i, **k),
                 reads, writes, dma=True)

        def pe_mm(outap, lhs, rhs, reads, writes, start=True, stop=True):
            S.op("pe", lambda e, o=outap, a=lhs, b=rhs, s1=start, s2=stop:
                 e.matmul(o, a, b, start=s1, stop=s2), reads, writes)

        def pe_tr(outap, inap, ident, reads, writes):
            S.op("pe", lambda e, o=outap, i=inap, d=ident: e.transpose(o, i, d),
                 reads, writes)

        def act(outap, inap, func, reads, writes, bias=None, scale=None, accum=None):
            kw = {}
            if bias is not None:
                kw["bias"] = bias
            if scale is not None:
                kw["scale"] = scale
            if accum is not None:
                kw["accum_out"] = accum
            S.op("act", lambda e, o=outap, i=inap, f=func, k=kw:
                 e.activation(o, i, f, **k), reads, writes)

        def tt(outap, a, bb, op, reads, writes):
            S.op("dve", lambda e, o=outap, x=a, y=bb, p=op: e.tensor_tensor(o, x, y, p),
                 reads, writes, aps=(outap, a, bb))

        def ts(outap, a, s1, s2, op0, op1, reads, writes):
            ptr = any(hasattr(v, "ap") for v in (s1, s2))
            S.op("dve", lambda e, o=outap, x=a, u=s1, v=s2, p0=op0, p1=op1:
                 e.tensor_scalar(o, x, u, v, p0, p1), reads, writes,
                 aps=() if ptr else (outap, a))

        def stt(outap, in0, scal, in1, op0, op1, reads, writes):
            ptr = hasattr(scal, "ap")
            S.op("dve", lambda e, o=outap, x=in0, s=scal, y=in1, p0=op0, p1=op1:
                 e.scalar_tensor_tensor(o, x, s, y, p0, p1), reads, writes,
                 aps=() if ptr else (outap, in0, in1))

        def gp_tt(outap, a, bb, op, reads, writes):
            S.op("gpsq", lambda e, o=outap, x=a, y=bb, p=op:
                 e.tensor_tensor(o, x, y, p), reads, writes)

        def cp(outap, inap, reads, writes, eng="dve"):
            S.op(eng, lambda e, o=outap, i=inap: e.tensor_copy(o, i), reads, writes,
                 aps=(outap, inap))

        def red(outap, inap, op, reads, writes):
            S.op("dve", lambda e, o=outap, i=inap, p=op:
                 e.tensor_reduce(o, i, mybir.AxisListType.X, p), reads, writes,
                 aps=(outap, inap))

        dbg_done = set()

        def tap(name, ap, reads):
            if not DEBUG or name in dbg_done:
                return
            dbg_done.add(name)
            shp = list(ap.shape)
            d = nc.dram_tensor("dbg_" + name, shp, ap.dtype, kind="ExternalOutput").ap()
            dma("syncq", d, ap, reads, ("dbg" + name,))

        for nm in smb_names:
            dma("syncq", bST[nm], dr[nm], (), (nm,))
        dma("syncq", bIDB, dr["c_id_bf"], (), ("idb",))
        dma("syncq", sIDF, dr["c_id_f32"], (), ("idf",))
        S.op("dve", lambda e: e.memset(bONESR, 1.0), (), ("onesr",))
        S.op("dve", lambda e: e.memset(bNEGH, -0.5), (), ("negh",))
        S.op("dve", lambda e: e.memset(bONESC, 1.0), (), ("onesc",))
        dma("gpsq", bROUT, dr["routf"].rearrange("p (k m) -> p k m", k=8),
            (), ("rout",))

        dma("syncq", SM[:, 0:416], dr["biasf"], (),
            ("ab", "pb", "fb", "mb", "lnv"))

        def top_ln_into(dst, dst_col0, ntiles, tok0, xn_tile_sink, tpx="I"):
            for tti in range(ntiles):
                xp = tti % 2
                xt = GENF[:, xp * 1024:(xp + 1) * 1024]
                xk = "xt%d" % xp
                dma("syncq", xt, dr["x"][tok0 + tti * 128: tok0 + (tti + 1) * 128, :],
                    (xk,), (xk,))
                red(sTOP[:, tti, 0:1], xt, Op.add, (xk,), ("top",))
                act(vXNT[:, 0, :], xt, AF.Square, (xk,), ("W1",))
                red(sTOP[:, tti, 1:2], vXNT[:, 0, :], Op.add, ("W1",), ("top",))
            vsum = sTOP[:, 0:ntiles, 0:1]
            vsq = sTOP[:, 0:ntiles, 1:2]
            ts(vsum, vsum, RINV, None, Op.mult, Op.bypass, ("top",), ("top",))
            ts(vsq, vsq, RINV, EPS, Op.mult, Op.add, ("top",), ("top",))
            tt(sTOPR[:, 0:ntiles], vsum, vsum, Op.mult, ("top",), ("rst",))
            tt(sTOPR[:, 0:ntiles], vsq, sTOPR[:, 0:ntiles], Op.subtract,
               ("top", "rst"), ("rst",))
            act(sTOPR[:, 0:ntiles], sTOPR[:, 0:ntiles], AF.Ln, ("rst",), ("rst",))
            act(sTOPR[:, 0:ntiles], sTOPR[:, 0:ntiles], AF.Exp, ("rst",), ("rst",),
                scale=-0.5)
            for tti in range(ntiles):
                xp = tti % 2
                xt = GENF[:, xp * 1024:(xp + 1) * 1024]
                xk = "xt%d" % xp
                dma("syncq", xt, dr["x"][tok0 + tti * 128: tok0 + (tti + 1) * 128, :],
                    (xk,), (xk,))
                ts(vXNT[:, 0, :], xt, sTOP[:, tti, 0:1], sTOPR[:, tti:tti + 1],
                   Op.subtract, Op.mult, (xk, "top", "rst"), ("W1",))
                for fc in range(8):
                    pe_tr(pTP[:, fc * 128:(fc + 1) * 128],
                          vXNT[:, 0, fc * 128:(fc + 1) * 128], bIDB,
                          ("W1", "idb"), ("pTP",))
                xn_tile_sink(tti, pTP[:, 0:1024].rearrange("p (a b) -> p a b", a=8))
            if DEBUG:
                tap("TOPST" + tpx, SM[:, 416:480], ("top",))
                tap("TOPR" + tpx, sTOPR[:, 0:32], ("rst",))

        def sink_R(tt, ptp_view):
            cp(R[:, :, tt * 128:(tt + 1) * 128], ptp_view, ("pTP",), ("R",))

        top_ln_into(None, None, 32, 0, sink_R)
        tap("R0", R[:, :, :], ("R",))

        def ln_stats(l, nch, ch, murow_o, rstdrow_o, sqbuf, sqkey, wrow_key):
            for c in range(nch):
                tok = slice(c * ch, (c + 1) * ch)
                act(sqbuf, R[:, :, tok], AF.Square, ("R",), (sqkey,))
                for k in range(8):
                    pe_mm(pSTAT[0:1, 0:ch], bONESC, R[:, k, tok],
                          ("R", "onesc"), ("pSTAT",), start=(k == 0), stop=(k == 7))
                for k in range(8):
                    pe_mm(pA[0:1, 0:ch], bONESC, sqbuf[:, k, :],
                          (sqkey, "onesc"), ("pA",), start=(k == 0), stop=(k == 7))
                mu = grow(murow_o + c * ch, ch)
                kmu, krs = wrow_key + "mu%d" % c, wrow_key + "rs%d" % c
                ts(mu, pSTAT[0:1, 0:ch], RINV, None, Op.mult, Op.bypass,
                   ("pSTAT",), (kmu,))
                ts(sVAR1[:, 0:ch], pA[0:1, 0:ch], RINV, EPS, Op.mult, Op.add,
                   ("pA",), ("var1",))
                tt(pSTAT[0:1, 0:ch], mu, mu, Op.mult, (kmu,), ("pSTAT",))
                rsr = grow(rstdrow_o + c * ch, ch)
                tt(rsr, sVAR1[:, 0:ch], pSTAT[0:1, 0:ch], Op.subtract,
                   ("var1", "pSTAT"), (krs,))
            allk = tuple(wrow_key + "rs%d" % i for i in range(nch))
            rall = grow(rstdrow_o, nch * ch)
            act(rall, rall, AF.Ln, allk, allk)
            act(rall, rall, AF.Exp, allk, allk, scale=-0.5)

        def ln_apply(l, tok, ch, murow_o, rstdrow_o, wrow_key, lnw, lnb, xh, lnt,
                     xh_key, row_off=None):
            ro = tok.start if row_off is None else row_off
            ci = ro // ch
            kmu, krs = wrow_key + "mu%d" % ci, wrow_key + "rs%d" % ci
            if ch <= 256 and ci % 2 == 1:
                ptpf = pTP[:, :].bitcast(F32)
                mub = ptpf[:, 0:ch]
                rstdb = ptpf[:, 256:256 + ch]
                mkey = rkey = "pTP"
            else:
                mub = pSTAT[:, 0:ch]
                rstdb = pSTAT[:, 256:256 + ch] if ch <= 256 else pA[:, 0:ch]
                mkey = "pSTAT"
                rkey = "pSTAT" if ch <= 256 else "pA"
            pe_mm(mub, bONESR, grow(murow_o + ro, ch), (kmu, "onesr"), (mkey,))
            pe_mm(rstdb, bONESR, grow(rstdrow_o + ro, ch), (krs, "onesr"), (rkey,))
            for t in range(8):
                tt(lnt[:, 0, :], R[:, t, tok], mub, Op.subtract,
                   ("R", mkey), ("lnt",))
                tt(lnt[:, 0, :], lnt[:, 0, :], rstdb, Op.mult, ("lnt", rkey), ("lnt",))
                ts(xh[:, t, :], lnt[:, 0, :], lnw[:, t:t + 1], lnb[:, t:t + 1],
                   Op.mult, Op.add, ("lnt", "lnv"), (xh_key,))

        def a1a(l):
            aw = dr["attn_w"][l].rearrange("(k p) (h three c) -> three k p h c",
                                           p=128, three=3, c=64)
            for k in range(8):
                dma("gpsq", vWQ[:, k, :], aw[0, k], ("WQ",), ("WQ",))
                dma("gpsq", vWK[:, k, :], aw[1, k], ("WK",), ("WK",))
                dma("gpsq", vWV[:, k, :], aw[2, k], ("WV",), ("WV",))
            ln_stats(l, NCH1, CH1, MUROW_O, RSTDROW_O, vZ, "Z", "w1")
            if DEBUG and l == 1:
                tap("W1MU", GEN[0:1, MUROW_O:MUROW_O + T],
                    tuple("w1mu%d" % i for i in range(NCH1)))
                tap("W1RS", GEN[0:1, RSTDROW_O:RSTDROW_O + T],
                    tuple("w1rs%d" % i for i in range(NCH1)))

            def P1(c):
                tok = slice(c * CH1, (c + 1) * CH1)
                ln_apply(l, tok, CH1, MUROW_O, RSTDROW_O, "w1",
                         sLN1W[:, l, :], sLN1B[:, l, :], vXH, vLNT, "xh")

            def P2(c):
                p = c % 2
                vQ, vKb, vVb = vQ2P[p], vK2P[p], vV2P[p]
                kq, kk_, kv = "Q%d" % p, "K%d" % p, "V%d" % p
                if c > 0:
                    po = (c - 1) % 2
                    for tnm, tb, hb, hk in (("Q%d" % po, vQ2P[po], vHQ, "HQ"),
                                            ("K%d" % po, vK2P[po], vHK, "HK"),
                                            ("V%d" % po, vV2P[po], vHV, "HV")):
                        cp(hb[:, :, 0:1], tb[:, :, CH1 - 1:CH1], (tnm,), (hk,))
                for tnm, tb, wv, wkey, qoff in ((kq, vQ, vWQ, "WQ", 0),
                                                (kk_, vKb, vWK, "WK", 1),
                                                (kv, vVb, vWV, "WV", 2)):
                    for mt in range(8):
                        bank = (qoff * 8 + mt) % 4
                        pview = pBIG[:, bank * 512: bank * 512 + CH1]
                        bk = "pBG%d" % bank
                        for k in range(8):
                            pe_mm(pview, wv[:, k, mt * 128:(mt + 1) * 128],
                                  vXH[:, k, :], ("xh", wkey), (bk,),
                                  start=(k == 0), stop=(k == 7))
                        act(tb[:, mt, 0:CH1], pview, AF.Identity,
                            (bk, "ab"), (tnm,), bias=sAB[:, l, qoff * 8 + mt:
                                                         qoff * 8 + mt + 1])
                if c == 0:
                    for tnm, tb, hb, hk in (("Q0", vQ2P[0], vHQ, "HQ"),
                                            ("K0", vK2P[0], vHK, "HK"),
                                            ("V0", vV2P[0], vHV, "HV")):
                        cp(hb[:, :, 0:1], tb[:, :, 0:1], (tnm,), (hk,))

            def P3(c):
                p = c % 2
                vQ, vKb, vVb = vQ2P[p], vK2P[p], vV2P[p]
                kq, kk_, kv = "Q%d" % p, "K%d" % p, "V%d" % p
                for s in range(2):
                    sr = slice(s * 4, s * 4 + 4)
                    for tnm, tb, rb, rk in ((kq, vQ, vRQ, "RQ"),
                                            (kk_, vKb, vRK, "RK"),
                                            (kv, vVb, vRV, "RV")):
                        dma("syncq", rb[64:128, :, :], tb[0:64, sr, :],
                            (tnm, rk), (rk,))
                        if s == 0:
                            dma("syncq", rb[0:64, 1:4, :],
                                tb[64:128, 0:3, :], (tnm, rk), (rk,))
                            dma("syncq", rb[0:64, 0:1, :],
                                tb[64:128, 7:8, :], (tnm, rk), (rk,))
                        else:
                            dma("syncq", rb[0:64, 0:4, :],
                                tb[64:128, 3:7, :], (tnm, rk), (rk,))
                    for tb, tnm, hb, hk, db, dk, cb, ck2 in (
                            (vQ, kq, vHQ, "HQ", vDQ, "DQ", vCQ, "CQ"),
                            (vKb, kk_, vHK, "HK", vDK, "DK", vCK, "CK"),
                            (vVb, kv, vHV, "HV", vDV, "DV", vCV, "CV")):
                        tt(db[:, :, 1:CH1], tb[:, sr, 1:CH1], tb[:, sr, 0:CH1 - 1],
                           Op.subtract, (tnm,), (dk,))
                        tt(db[:, :, 0:1], tb[:, sr, 0:1], hb[:, sr, :],
                           Op.subtract, (tnm, hk), (dk,))
                        ct0 = vZDRE[:, :, :]
                        ct1 = vZDIM[:, :, :]
                        act(ct0, tb[:, sr, :], AF.Sin, (tnm,), ("ZDRE",),
                            scale=0.5)
                        act(ct1, ct0, AF.Square, ("ZDRE",), ("ZDIM",),
                            scale=float(np.sqrt(2.0)))
                        ts(cb, ct1, 1.0, -1.0, Op.subtract, Op.mult, ("ZDIM",), (ck2,))
                    qs = (vQ[:, sr, :], vDQ, vRQ, vCQ)
                    kk = (vKb[:, sr, :], vDK, vRK, vCK)
                    vv = (vVb[:, sr, :], vDV, vRV, vCV)
                    qr = (kq, "DQ", "RQ", "CQ")
                    kr = (kk_, "DK", "RK", "CK")
                    vr = (kv, "DV", "RV", "CV")
                    terms = [
                        [(0, 0, 1), (1, 1, -1), (2, 2, -1), (3, 3, -1)],
                        [(0, 1, 1), (1, 0, 1), (2, 3, 1), (3, 2, -1)],
                        [(0, 2, 1), (2, 0, 1), (1, 3, -1), (3, 1, 1)],
                        [(0, 3, 1), (3, 0, 1), (1, 2, 1), (2, 1, -1)],
                    ]
                    zsl = vZ[:, sr, :]
                    for i in range(4):
                        first = True
                        for (a, bq, sgn) in terms[i]:
                            if first:
                                tt(vT0, qs[a], kk[bq], Op.mult,
                                   (qr[a], kr[bq]), ("T0",))
                                first = False
                            else:
                                tt(vT1, qs[a], kk[bq], Op.mult,
                                   (qr[a], kr[bq]), ("T1",))
                                tt(vT0, vT0, vT1,
                                   Op.add if sgn > 0 else Op.subtract,
                                   ("T0", "T1"), ("T0",))
                        tt(vT1, vT0, vv[i], Op.mult, ("T0", vr[i]), ("T1",))
                        if i == 0:
                            cp(zsl, vT1, ("T1",), ("Z",))
                        else:
                            tt(zsl, zsl, vT1, Op.subtract, ("Z", "T1"), ("Z",))

            def P4(c):
                for half in range(2):
                    for t4 in range(4):
                        t = half * 4 + t4
                        pe_mm(pA[:, 0:CH1], bST["c_sd_re"], vZ[:, t, :],
                              ("Z", "c_sd_re"), ("pA",))
                        pe_mm(pB[:, 0:CH1], bST["c_sd_im"], vZ[:, t, :],
                              ("Z", "c_sd_im"), ("pB",))
                        act(vZDRE[:, t4, :], pA[:, 0:CH1], AF.Identity,
                            ("pA",), ("ZDRE",))
                        act(vZDIM[:, t4, :], pB[:, 0:CH1], AF.Identity,
                            ("pB",), ("ZDIM",))
                    hs = slice(half * 4, half * 4 + 4)
                    dma("syncq", bounce[0, 0, c][:, :, hs, :], vZDRE,
                        ("ZDRE",), ("bnc0",))
                    dma("syncq", bounce[0, 1, c][:, :, hs, :], vZDIM,
                        ("ZDIM",), ("bnc0",))
                if DEBUG and l == 0 and c == 0:
                    tap("XH", vXH, ("xh",))
                    tap("Z", vZ, ("Z",))

            P1(0); P2(0)
            for c in range(NCH1):
                if c + 1 < NCH1:
                    P1(c + 1)
                P3(c)
                if c + 1 < NCH1:
                    P2(c + 1)
                P4(c)

        def a1b(l):
            dma("gpsq", vWP, dr["aproj_w"][l].rearrange("(k p) m -> p k m", p=128),
                ("WP",), ("WP",))
            dma("gpsq", GEN[0:1, MODROW_O:MODROW_O + T], dr["c_mod"],
                ("modr",), ("modr",))

            def Q1(c):
                p = c % 2
                tok = slice(c * CH1, (c + 1) * CH1)
                dma("syncq", vFC2P[p], dr["c_filt_c"][:, :, tok],
                    ("FC%d" % p,), ("FC%d" % p,))
                dma("syncq", vFS2P[p], dr["c_filt_s"][:, :, tok],
                    ("FS%d" % p,), ("FS%d" % p,))
                for A in range(8):
                    dma("syncq", vLHRE2P[p][:, A, :],
                        bounce[0, 0, c][:, A].rearrange("sg t tok -> t sg tok"),
                        ("bnc0", "LHRE%d" % p), ("LHRE%d" % p,))
                    dma("syncq", vLHIM2P[p][:, A, :],
                        bounce[0, 1, c][:, A].rearrange("sg t tok -> t sg tok"),
                        ("bnc0", "LHIM%d" % p), ("LHIM%d" % p,))

            def Q2(c):
                p = c % 2
                lre, lim = vLHRE2P[p], vLHIM2P[p]
                kre, kim = "LHRE%d" % p, "LHIM%d" % p
                for ap_ in range(4):
                    asl = slice(2 * ap_, 2 * ap_ + 2)
                    pe_mm(pA[:, :], bST["c_sh_re"], lre[:, asl, :],
                          (kre, "c_sh_re"), ("pA",), start=True, stop=False)
                    pe_mm(pA[:, :], bST["c_sh_imn"], lim[:, asl, :],
                          (kim, "c_sh_imn"), ("pA",), start=False, stop=True)
                    pe_mm(pB[:, :], bST["c_sh_im"], lre[:, asl, :],
                          (kre, "c_sh_im"), ("pB",), start=True, stop=False)
                    pe_mm(pB[:, :], bST["c_sh_re"], lim[:, asl, :],
                          (kim, "c_sh_re"), ("pB",), start=False, stop=True)
                    pav = pA[:, :].rearrange("p (a b) -> p a b", a=2)
                    pbv = pB[:, :].rearrange("p (a b) -> p a b", a=2)
                    tt(vZPRE[:, asl, :], pav, vFC2P[p][:, asl, :], Op.mult,
                       ("pA", "FC%d" % p), ("ZPRE",))
                    tt(vFTMP, pbv, vFS2P[p][:, asl, :], Op.mult,
                       ("pB", "FS%d" % p), ("FT",))
                    tt(vZPRE[:, asl, :], vZPRE[:, asl, :], vFTMP, Op.subtract,
                       ("ZPRE", "FT"), ("ZPRE",))
                    tt(vZPIM[:, asl, :], pav, vFS2P[p][:, asl, :], Op.mult,
                       ("pA", "FS%d" % p), ("ZPIM",))
                    tt(vFTMP, pbv, vFC2P[p][:, asl, :], Op.mult,
                       ("pB", "FC%d" % p), ("FT",))
                    tt(vZPIM[:, asl, :], vZPIM[:, asl, :], vFTMP, Op.add,
                       ("ZPIM", "FT"), ("ZPIM",))
                    hire = pBIG[:, 0:512]
                    hiim = pBIG[:, 512:1024]
                    pe_mm(hire, bST["c_shi_re"], vZPRE[:, asl, :],
                          ("ZPRE", "c_shi_re"), ("pBG0",), start=True, stop=False)
                    pe_mm(hire, bST["c_shi_imn"], vZPIM[:, asl, :],
                          ("ZPIM", "c_shi_imn"), ("pBG0",), start=False, stop=True)
                    pe_mm(hiim, bST["c_shi_im"], vZPRE[:, asl, :],
                          ("ZPRE", "c_shi_im"), ("pBG1",), start=True, stop=False)
                    pe_mm(hiim, bST["c_shi_re"], vZPIM[:, asl, :],
                          ("ZPIM", "c_shi_re"), ("pBG1",), start=False, stop=True)
                    hirev = hire.rearrange("p (a b) -> p a b", a=2)
                    hiimv = hiim.rearrange("p (a b) -> p a b", a=2)
                    act(vYRE[:, asl, :], hirev, AF.Identity, ("pBG0",), ("YRE",))
                    act(vYIM[:, asl, :], hiimv, AF.Identity, ("pBG1",), ("YIM",))

            def Q3w(c):
                dma("syncq", bounce[1, 0, c], vYRE, ("YRE",), ("bnc1",))
                dma("syncq", bounce[1, 1, c], vYIM, ("YIM",), ("bnc1",))

            def Q3r(c):
                p = c % 2
                for A in range(8):
                    dma("syncq", vLHRE2P[p][:, A, :],
                        bounce[1, 0, c][:, A].rearrange("sg t tok -> t sg tok"),
                        ("bnc1", "LHRE%d" % p), ("LHRE%d" % p,))
                    dma("syncq", vLHIM2P[p][:, A, :],
                        bounce[1, 1, c][:, A].rearrange("sg t tok -> t sg tok"),
                        ("bnc1", "LHIM%d" % p), ("LHIM%d" % p,))

            def Q4(c):
                p = c % 2
                tok = slice(c * CH1, (c + 1) * CH1)
                yb = vYOUT2P[p]
                ybk = "YOUT%d" % p
                pe_mm(pSTAT[:, 0:CH1], bONESR, grow(MODROW_O + c * CH1, CH1),
                      ("modr", "onesr"), ("pSTAT",))
                act(vMODB[:, 0, :], pSTAT[:, 0:CH1], AF.Identity,
                    ("pSTAT",), ("MODB",))
                for t in range(8):
                    if t % 2 == 0:
                        pview = pBIG[:, 1536:1536 + CH1]
                        pk2 = "pBG3"
                    else:
                        pview = pSTAT[:, 0:CH1]
                        pk2 = "pSTAT"
                    pe_mm(pview, bST["c_sdi_re"], vLHRE2P[p][:, t, :],
                          ("LHRE%d" % p, "c_sdi_re"), (pk2,),
                          start=True, stop=False)
                    pe_mm(pview, bST["c_sdi_imn"], vLHIM2P[p][:, t, :],
                          ("LHIM%d" % p, "c_sdi_imn"), (pk2,),
                          start=False, stop=True)
                    tt(yb[:, t, :], pview, vMODB[:, 0, :], Op.mult,
                       (pk2, "MODB"), (ybk,))
                for m in range(8):
                    bank = 2 + (m % 2)
                    pview = pBIG[:, bank * 512: bank * 512 + CH1]
                    bk = "pBG%d" % bank
                    for k in range(8):
                        pe_mm(pview, vWP[:, k, m * 128:(m + 1) * 128], yb[:, k, :],
                              (ybk, "WP"), (bk,), start=(k == 0), stop=(k == 7))
                    stt(R[:, m, tok], pview, sPB[:, l, m:m + 1], R[:, m, tok],
                        Op.add, Op.add, (bk, "pb", "R"), ("R",))
                if DEBUG and l == 0 and c == 0:
                    tap("LHRE", vLHRE2P[0], ("LHRE0",))
                    tap("ZPRE", vZPRE, ("ZPRE",))
                    tap("YOUT", vYOUT2P[0], ("YOUT0",))

            Q1(0)
            if NCH1 > 1:
                Q1(1)
            for c in range(NCH1):
                Q2(c)
                Q3w(c)
                if c + 2 < NCH1:
                    Q1(c + 2)
            for c in range(NCH1):
                Q3r(c)
                Q4(c)

        def ln_stats2(l, th):
            for c2 in range(NCH2 // 2):
                tok = slice(th * THALF + c2 * CH2, th * THALF + (c2 + 1) * CH2)
                sqb = vFCW[:, 0:8, 0:CH2]
                act(sqb, R[:, :, tok], AF.Square, ("R", "FCW"), ("FCW",))
                for k in range(8):
                    pe_mm(pSTAT[0:1, 0:CH2], bONESC, R[:, k, tok],
                          ("R", "onesc"), ("pSTAT",), start=(k == 0), stop=(k == 7))
                for k in range(8):
                    pe_mm(pA[0:1, 0:CH2], bONESC, sqb[:, k, :],
                          ("FCW", "onesc"), ("pA",), start=(k == 0), stop=(k == 7))
                mu = grow(MUROW2_O + c2 * CH2, CH2)
                kmu, krs = "w2mu%d" % c2, "w2rs%d" % c2
                ts(mu, pSTAT[0:1, 0:CH2], RINV, None, Op.mult, Op.bypass,
                   ("pSTAT",), (kmu,))
                ts(sVAR1[:, 0:CH2], pA[0:1, 0:CH2], RINV, EPS, Op.mult, Op.add,
                   ("pA",), ("var1",))
                tt(pSTAT[0:1, 0:CH2], mu, mu, Op.mult, (kmu,), ("pSTAT",))
                rsr = grow(RSTDROW2_O + c2 * CH2, CH2)
                tt(rsr, sVAR1[:, 0:CH2], pSTAT[0:1, 0:CH2], Op.subtract,
                   ("var1", "pSTAT"), (krs,))
            allk = tuple("w2rs%d" % i for i in range(NCH2 // 2))
            rall = grow(RSTDROW2_O, THALF)
            act(rall, rall, AF.Ln, allk, allk)
            act(rall, rall, AF.Exp, allk, allk, scale=-0.5)

        def a2(l):
            for th in range(2):
                ln_stats2(l, th)
                if DEBUG and l == 0 and th == 0:
                    tap("W2MU", GEN[0:1, MUROW2_O:MUROW2_O + THALF],
                        tuple("w2mu%d" % i for i in range(4)))
                    tap("W2RS", GEN[0:1, RSTDROW2_O:RSTDROW2_O + THALF],
                        tuple("w2rs%d" % i for i in range(4)))
                for cc in range(NCH2 // 2):
                    tok = slice(th * THALF + cc * CH2, th * THALF + (cc + 1) * CH2)
                    xsl = slice(cc * CH2, (cc + 1) * CH2)
                    ln_apply(l, tok, CH2, MUROW2_O, RSTDROW2_O, "w2",
                             sLN2W[:, l, :], sLN2B[:, l, :],
                             vXH2H[:, :, xsl], vLNT2, "xh2", row_off=xsl.start)
                for hh in range(2):
                    dma("gpsq", vFCW,
                        dr["fc_w"][l][:, hh * 2048:(hh + 1) * 2048]
                        .rearrange("(k p) m -> p k m", p=128), ("FCW",), ("FCW",))
                    dma("gpsq", vMPW,
                        dr["mproj_w"][l][hh * 2048:(hh + 1) * 2048, :]
                        .rearrange("(k p) m -> p k m", p=128), ("MPW",), ("MPW",))
                    for cc in range(NCH2 // 2):
                        tok = slice(th * THALF + cc * CH2, th * THALF + (cc + 1) * CH2)
                        xsl = slice(cc * CH2, (cc + 1) * CH2)
                        for hm in range(16):
                            pv = (pA if hm % 2 == 0 else pB)[:, :]
                            pk = "pA" if hm % 2 == 0 else "pB"
                            for k in range(8):
                                pe_mm(pv, vFCW[:, k, hm * 128:(hm + 1) * 128],
                                      vXH2H[:, k, xsl], ("xh2", "FCW"), (pk,),
                                      start=(k == 0), stop=(k == 7))
                            act(vH1[:, hm, :], pv, AF.Gelu, (pk, "fb"),
                                ("H1",) + tuple("w2mu%d" % i for i in range(4))
                                + tuple("w2rs%d" % i for i in range(4)),
                                bias=sFB[:, l, hh * 16 + hm: hh * 16 + hm + 1])
                        for mh in range(2):
                            for m4 in range(4):
                                m = mh * 4 + m4
                                pview = pBIG[:, m4 * 512: (m4 + 1) * 512]
                                bk = "pBG%d" % m4
                                for k in range(16):
                                    pe_mm(pview, vMPW[:, k, m * 128:(m + 1) * 128],
                                          vH1[:, k, :], ("H1", "MPW"), (bk,),
                                          start=(k == 0), stop=(k == 15))
                                if hh == 0:
                                    stt(R[:, m, tok], pview, sMB[:, l, m:m + 1],
                                        R[:, m, tok], Op.add, Op.add,
                                        (bk, "mb", "R"), ("R",))
                                else:
                                    tt(R[:, m, tok], R[:, m, tok], pview, Op.add,
                                       ("R", bk), ("R",))

        def fence():
            S.fence(lambda e: e.memset(SMB[0:1, 2120:2121], 0.0))

        for l in range(L):
            a1a(l)
            fence()
            a1b(l)
            tap("Rattn%d" % l, R[:, :, :], ("R",))
            fence()
            a2(l)
            tap("Rlayer%d" % l, R[:, :, :], ("R",))
            fence()

        def sink_xnh(tt, ptp_view, th):
            cp(vXNH[:, :, tt * 128:(tt + 1) * 128], ptp_view, ("pTP",), ("XNH",))

        outk = []
        for th in range(2):
            tok0 = th * THALF
            top_ln_into(None, None, 16, tok0, lambda tt, v: sink_xnh(tt, v, th),
                        tpx="B%d" % th)
            for cc in range(THALF // 512):
                for k in range(8):
                    pe_mm(pSTAT[0:8, :], bROUT[:, k, :],
                          vXNH[:, k, cc * 512:(cc + 1) * 512],
                          ("XNH", "rout"), ("pSTAT",), start=(k == 0), stop=(k == 7))
                act(sLGT, pSTAT[0:8, :], AF.Identity, ("pSTAT",), ("lgt",))
                for bb in range(4):
                    g = cc * 4 + bb
                    pe_tr(pA[:, 0:8], sLGT[:, bb * 128:(bb + 1) * 128], sIDF[0:8, 0:8],
                          ("lgt", "idf"), ("pA",))
                    cp(sLTOK[:, g, :], pA[:, 0:8], ("pA",), ("ltok",))
            act(SM[:, 496:624], SM[:, 496:624], AF.Exp, ("ltok",), ("ltok",))
            red(sSUMS, sLTOK, Op.add, ("ltok",), ("sums",))
            S.op("dve", lambda e: e.reciprocal(sSUMS, sSUMS), ("sums",), ("sums",))
            for g in range(16):
                ts(sCOEFP[:, g, :], sLTOK[:, g, :], sSUMS[:, g:g + 1], None,
                   Op.mult, Op.bypass, ("ltok", "sums"), ("cfp",))
            red(sM1, sCOEFP, Op.max, ("cfp",), ("m1",))
            for g in range(16):
                ts(sSUP, sCOEFP[:, g, :], sM1[:, g:g + 1], -1e9, Op.is_ge, Op.mult,
                   ("cfp", "m1"), ("sup",))
                tt(sSUP, sCOEFP[:, g, :], sSUP, Op.add, ("cfp", "sup"), ("sup",))
                red(sM2[:, g:g + 1], sSUP, Op.max, ("sup",), ("m2",))
            for g in range(16):
                ts(sSUP, sCOEFP[:, g, :], sM2[:, g:g + 1], None, Op.is_ge, Op.bypass,
                   ("cfp", "m2"), ("sup",))
                tt(vCOEFT[:, g, :], sCOEFP[:, g, :], sSUP, Op.mult,
                   ("cfp", "sup"), ("W2",))
            for g in range(16):
                pe_tr(pTP[0:8, 0:128], vCOEFT[:, g, :], bIDB, ("W2", "idb"), ("pTP",))
                cp(vCOEFROW[:, g * 128:(g + 1) * 128], pTP[0:8, 0:128],
                   ("pTP",), ("crow",))
            if th == 0:
                tap("XNH", vXNH, ("XNH",))
                tap("COEFROW", GEN[0:8, 57344:57344 + 2048], ("crow",))
            dma("gpsq", vSGW, dr["sg"].rearrange("(k p) m -> p k m", p=128),
                ("EW0",), ("EW0",))
            dma("gpsq", vSUW, dr["su"].rearrange("(k p) m -> p k m", p=128),
                ("EW0", "EW1"), ("EW0", "EW1"))
            dma("gpsq", vSDW, dr["sd"].rearrange("(k p) m -> p k m", p=128),
                ("EW1",), ("EW1",))
            nchb = THALF // CHB
            for cc in range(nchb):
                tok = slice(tok0 + cc * CHB, tok0 + (cc + 1) * CHB)
                xsl = slice(cc * CHB, (cc + 1) * CHB)
                for mh in range(2):
                    for j in range(4):
                        m = mh * 4 + j
                        pv = (pA if j % 2 == 0 else pB)[:, (j // 2) * 256:
                                                        (j // 2) * 256 + CHB]
                        pk = "pA" if j % 2 == 0 else "pB"
                        for k in range(8):
                            pe_mm(pv, vSGW[:, k, m * 128:(m + 1) * 128],
                                  vXNH[:, k, xsl], ("XNH", "EW0"), (pk,),
                                  start=(k == 0), stop=(k == 7))
                        act(vSGSH[:, j, :], pv, AF.Silu, (pk,), ("W1",))
                        for k in range(8):
                            pe_mm(pv, vSUW[:, k, m * 128:(m + 1) * 128],
                                  vXNH[:, k, xsl], ("XNH", "EW0", "EW1"), (pk,),
                                  start=(k == 0), stop=(k == 7))
                        tt(vHUSH[:, m, :], pv, vSGSH[:, j, :], Op.mult,
                           (pk, "W1"), ("W2", "W3"))
                for m in range(8):
                    pview = pBIG[:, m * 256: m * 256 + CHB]
                    bk = "pBG%d" % (m // 2)
                    for k in range(8):
                        pe_mm(pview, vSDW[:, k, m * 128:(m + 1) * 128],
                              vHUSH[:, k, :], ("W2", "W3", "EW1"), (bk,),
                              start=(k == 0), stop=(k == 7))
                act(vACC[:, :, xsl],
                    pBIG[:, :].rearrange("p (a b) -> p a b", a=8), AF.Identity,
                    ("pBG0", "pBG1", "pBG2", "pBG3"), ("ACC",))
            for e_i in range(NEXP):
                o = EW_O[e_i % 2]
                vGW = gv(o, 8, 512); vUW = gv(o + 4096, 8, 512)
                vDW = gv(o + 8192, 4, 1024)
                wk = "EW%d" % (e_i % 2)
                dma("gpsq", vGW, dr["eg"][e_i].rearrange("(k p) m -> p k m", p=128),
                    (wk,), (wk,))
                dma("gpsq", vUW, dr["eu"][e_i].rearrange("(k p) m -> p k m", p=128),
                    (wk,), (wk,))
                dma("gpsq", vDW, dr["ed"][e_i].rearrange("(k p) m -> p k m", p=128),
                    (wk,), (wk,))
                for cc in range(nchb // 2):
                    xsl = slice(cc * 512, (cc + 1) * 512)
                    dma("syncq", vCOEFR0B, vCOEFROW[e_i:e_i + 1, xsl],
                        ("crow", "W3"), ("W3",))
                    pe_mm(pSTAT[:, 0:512], bONESR, vCOEFR0B,
                          ("W3", "onesr"), ("pSTAT",))
                    for j in range(4):
                        pv = pBIG[:, j * 512:(j + 1) * 512]
                        pk = "pBG%d" % j
                        for k in range(8):
                            pe_mm(pv, vGW[:, k, j * 128:(j + 1) * 128],
                                  vXNH[:, k, xsl], ("XNH", wk), (pk,),
                                  start=(k == 0), stop=(k == 7))
                        act(vSGE[:, j % 2, :], pv, AF.Silu, (pk,), ("W1%d" % (j % 2),))
                        for k in range(8):
                            pe_mm(pv, vUW[:, k, j * 128:(j + 1) * 128],
                                  vXNH[:, k, xsl], ("XNH", wk), (pk,),
                                  start=(k == 0), stop=(k == 7))
                        tt(vHUE[:, j, :], pv, vSGE[:, j % 2, :], Op.mult,
                           (pk, "W1%d" % (j % 2)), ("W2",))
                        tt(vHUE[:, j, :], vHUE[:, j, :], pSTAT[:, 0:512], Op.mult,
                           ("W2", "pSTAT"), ("W2",))
                    for mq in range(4):
                        for m2 in range(2):
                            m = mq * 2 + m2
                            pview = (pA if m2 == 0 else pB)[:, :]
                            bk = "pA" if m2 == 0 else "pB"
                            for k in range(4):
                                pe_mm(pview, vDW[:, k, m * 128:(m + 1) * 128],
                                      vHUE[:, k, :], ("W2", wk), (bk,),
                                      start=(k == 0), stop=(k == 3))
                            tt(vACC[:, m, xsl], vACC[:, m, xsl], pview, Op.add,
                               ("ACC", bk), ("ACC",))
            if th == 0:
                tap("ACC", vACC, ("ACC",))
            for tt_i in range(THALF // 128):
                xt = GENF[:, 0:1024]
                ot = GENF[:, 1024:2048]
                tsl = slice(tt_i * 128, (tt_i + 1) * 128)
                gsl = slice(tok0 + tt_i * 128, tok0 + (tt_i + 1) * 128)
                dma("syncq", xt, dr["x"][gsl, :], ("xt0", "xt1", "ot"), ("xt0",))
                for fc in range(8):
                    act(vFSIL[:, 0, :], vACC[:, fc, tsl], AF.Silu, ("ACC",), ("W2",))
                    tt(vFPROD[:, 0, :], vFSIL[:, 0, :], R[:, fc, gsl], Op.mult,
                       ("W2", "R"), ("W2",))
                    pe_tr(pTP[:, fc * 128:(fc + 1) * 128], vFPROD[:, 0, :], bIDB,
                          ("W2", "idb"), ("pTP",))
                tt(ot, pTP[:, 0:1024], xt, Op.add, ("pTP", "xt0"), ("ot", "xt1"))
                ok = "out%d_%d" % (th, tt_i)
                outk.append(ok)
                dma("syncq", out_dr[gsl, :], ot, ("ot",), (ok,))
            fence()

        S.op("dve", lambda e: e.memset(SMB[0:1, 2121:2122], 0.0), tuple(outk), ())

        emit_lists, final_counts = S.finalize()

        def run_engine(e, name):
            for wl, o in emit_lists[name]:
                for (se, v) in wl:
                    e.wait_ge(sems[se], v)
                ins = o.emit(e)
                if o.need_inc:
                    ins.then_inc(sems[o.sem], o.inc)

        @block.tensor
        def _(e):
            run_engine(e, "pe")

        @block.vector
        def _(e):
            run_engine(e, "dve")

        @block.scalar
        def _(e):
            run_engine(e, "act")

        @block.sync
        def _(e):
            run_engine(e, "syncq")

        @block.gpsimd
        def _(e):
            run_engine(e, "gpsq")

    _PROG = nc
    return nc


def kernel(**inputs):
    cs = host_constants()
    nc = build_program()
    x = np.ascontiguousarray(np.asarray(inputs["L0_attn_x"]))
    base = {
        "ln1_w": inputs["L0_attn_ln1_w"], "ln1_b": inputs["L0_attn_ln1_b"],
        "attn_w": inputs["L0_attn_attn_w"], "attn_b": inputs["L0_attn_attn_b"],
        "aproj_w": inputs["L0_attn_aproj_w"], "aproj_b": inputs["L0_attn_aproj_b"],
        "ln2_w": inputs["L0_attn_ln2_w"], "ln2_b": inputs["L0_attn_ln2_b"],
        "fc_w": inputs["L0_attn_fc_w"], "fc_b": inputs["L0_attn_fc_b"],
        "mproj_w": inputs["L0_attn_mproj_w"], "mproj_b": inputs["L0_attn_mproj_b"],
        "router": inputs["L0_ffn_router"],
        "eg": inputs["L0_ffn_experts_gate"], "eu": inputs["L0_ffn_experts_up"],
        "ed": inputs["L0_ffn_experts_down"],
        "sg": inputs["L0_ffn_shared_gate"], "su": inputs["L0_ffn_shared_up"],
        "sd": inputs["L0_ffn_shared_down"],
    }
    base = {k: np.ascontiguousarray(np.asarray(v, dtype=np.float32))
            for k, v in base.items()}
    def fmaj(v):
        return np.asarray(v, np.float32).reshape(8, 128).T
    cols = []
    ab = np.asarray(base["attn_b"], np.float32).reshape(L, 16, 3, 64)
    cols.append(np.concatenate(
        [np.concatenate([ab[l, :, q, :].reshape(8, 128).T for q in range(3)], 1)
         for l in range(L)], 1))
    cols.append(np.concatenate([fmaj(base["aproj_b"][l]) for l in range(L)], 1))
    fb = np.asarray(base["fc_b"], np.float32).reshape(L, 32, 128)
    cols.append(np.concatenate([fb[l].T for l in range(L)], 1))
    cols.append(np.concatenate([fmaj(base["mproj_b"][l]) for l in range(L)], 1))
    for nmv in ("ln1_w", "ln1_b", "ln2_w", "ln2_b"):
        cols.append(np.concatenate([fmaj(base[nmv][l]) for l in range(L)], 1))
    base["biasf"] = np.ascontiguousarray(np.concatenate(cols, axis=1))
    assert base["biasf"].shape == (128, 416)
    base["routf"] = np.ascontiguousarray(
        np.asarray(base["router"], np.float32).reshape(8, 128, 8)
        .transpose(1, 0, 2).reshape(128, 64))
    base.update(cs)
    in_maps = []
    for c in range(NCORE):
        m = dict(base)
        m["x"] = np.ascontiguousarray(x[c])
        in_maps.append(m)
    res = run_bass_kernel_spmd(nc, in_maps, core_ids=list(range(NCORE)))
    out = np.stack([res.results[c]["out"] for c in range(NCORE)], axis=0)
    return out.astype(np.float32)
